# version 18
# baseline (speedup 1.0000x reference)
import numpy as np
import ml_dtypes
from contextlib import ExitStack
import concourse.bass as bass
import concourse.mybir as mybir
from concourse.bass_utils import run_bass_kernel_spmd

F32 = mybir.dt.float32
BF16 = mybir.dt.bfloat16
AF = mybir.ActivationFunctionType
ALU = mybir.AluOpType
AX = mybir.AxisListType

NTOK = 4096
DM = 1024
E = 8
FF = 2048
NSUB = NTOK // 128
WAVE = 2048
NW = NTOK // WAVE
NQ = WAVE // 512
NCF = FF // 512
EPS_LN = 1e-5
KVW = 66

TOP = 229344
A0 = TOP - 65536
B0 = A0 - 65536
C0 = B0 - 61440
CW = C0 + 36864

T, V, S, SY, GP = "tensor", "vector", "scalar", "sync", "gpsimd"
ENGS = [T, V, S, SY, GP]


class Sem:
    def __init__(self, nc, es, name):
        self.h = es.enter_context(nc.semaphore(name))
        self.v = 0
        self.name = name


class Prog:
    def __init__(self):
        self.ops = {e: [] for e in ENGS}

    def emit(self, eng, fn):
        self.ops[eng].append(fn)

    def wait(self, eng, sem, val=None):
        v = sem.v if val is None else val
        if v > 0:
            self.ops[eng].append(lambda e, h=sem.h, v=v: e.wait_ge(h, v))

    def run(self, eng, engine_obj):
        for fn in self.ops[eng]:
            fn(engine_obj)


def build_program():
    nc = bass.Bass("TRN2", target_bir_lowering=False, debug=False)
    assert nc.sbuf_top == TOP, nc.sbuf_top

    x_d = nc.dram_tensor("x", [NTOK, DM], F32, kind="ExternalInput").ap()
    wqkv_d = nc.dram_tensor("w_qkv", [DM, 3 * DM], F32, kind="ExternalInput").ap()
    bqkv_d = nc.dram_tensor("b_qkv", [3 * DM], F32, kind="ExternalInput").ap()
    wproj_d = nc.dram_tensor("w_proj", [DM, DM], F32, kind="ExternalInput").ap()
    bproj_d = nc.dram_tensor("b_proj", [DM], F32, kind="ExternalInput").ap()
    rw_d = nc.dram_tensor("router_w", [E, DM], F32, kind="ExternalInput").ap()
    rb_d = nc.dram_tensor("router_b", [E], F32, kind="ExternalInput").ap()
    temp_d = nc.dram_tensor("temperature", [1], F32, kind="ExternalInput").ap()
    wg_d = nc.dram_tensor("wg", [E, DM, FF], F32, kind="ExternalInput").ap()
    wp_d = nc.dram_tensor("wp", [E, DM, FF], F32, kind="ExternalInput").ap()
    wo_d = nc.dram_tensor("wo", [E, FF, DM], F32, kind="ExternalInput").ap()
    identb_d = nc.dram_tensor("identb", [128, 128], BF16, kind="ExternalInput").ap()
    out_d = nc.dram_tensor("out", [NTOK, DM], F32, kind="ExternalOutput").ap()
    base_d = nc.dram_tensor("base_scr", [NTOK, DM], F32, kind="Internal").ap()

    es = ExitStack()
    sb = lambda name, shape, dt=F32: es.enter_context(nc.sbuf_tensor(name, shape, dt))
    man = lambda name, shape, dt, off: nc.alloc_sbuf_tensor_at(name, shape, dt, offset=off)

    identb = sb("identb_s", [128, 128], BF16)
    rwT = sb("rwT", [128, 8 * E], BF16)
    tmp1 = sb("tmp1", [1, 4])
    rtA = sb("rtA", [128, 2])
    rtemp = sb("rtemp", [128, 1])
    bqkv_row = sb("bqkv_row", [1, 3 * DM], BF16)
    bproj_row = sb("bproj_row", [1, DM], BF16)
    rb_row = sb("rb_row", [1, E], BF16)
    onesrow = sb("onesrow", [1, 512], BF16)
    onesf1 = sb("onesf1", [1, 128])
    onesb = sb("onesb", [128, 1], BF16)
    onesf8 = sb("onesf8", [8, 128])
    onescol = sb("onescol", [128, 1])
    eye8 = sb("eye8", [8, 8])
    kvbd = sb("kvbd", [128, 8 * 130], BF16)
    acH = sb("acH", [128, DM], BF16)
    scores = sb("scores", [128, 256])
    wsel = sb("wsel", [128, 256])
    skE = sb("skE", [128, 256])
    sk2 = sb("sk2", [128, 256])
    sk32 = sb("sk32", [128, 32])
    sk32b = sb("sk32b", [128, 32])
    sk32c = sb("sk32c", [128, 32])
    sk32d = sb("sk32d", [128, 32])
    sk8 = sb("sk8", [128, 8])
    skL8 = sb("skL8", [8, 1])
    skL8b = sb("skL8b", [8, 1])
    skdg = sb("skdg", [8, 8])
    z16b = sb("z16b", [128, 16])
    z16c = sb("z16c", [128, 16])
    lnc = sb("lnc", [128, 16])
    mv12 = sb("mv12", [128, 24])
    mv2 = sb("mv2", [128, 4])

    q_sb = man("q_sb", [128, 8 * NTOK], BF16, A0)
    out_acc = man("out_acc", [128, 16 * DM], F32, A0)
    wqkv_sb = man("wqkv_sb", [128, 8 * 3072], BF16, B0)
    xt_sb = [man(f"xt{i}", [128, DM], F32, B0 + 49152 + 4096 * i) for i in range(2)]
    h2T = man("h2T", [128, 8 * NTOK], BF16, B0)
    h1b = [man("h1b0", [128, DM], BF16, C0), man("h1b1", [128, DM], BF16, C0 + 26624)]
    h1T = man("h1T", [128, 8 * 512], BF16, C0 + 2048)
    kb_bf = [man(f"kb{i}", [128, DM], BF16, C0 + 10240 + 2048 * i) for i in range(2)]
    vb_bf = [man(f"vb{i}", [128, DM], BF16, C0 + 14336 + 2048 * i) for i in range(2)]
    qtmp1 = man("qtmp1", [128, 512], F32, C0 + 18432)
    qtmp2 = man("qtmp2", [128, 512], F32, C0 + 20480)
    qtmp3 = man("qtmp3", [128, 512], F32, C0 + 22528)
    rws = man("rws", [E, DM], BF16, C0 + 24576)
    assert C0 + 26624 <= CW
    wproj_sb = man("wproj_sb", [128, 8 * DM], BF16, C0)
    xt2_sb = [man(f"xt2_{i}", [128, DM], F32, C0 + 16384 + 4096 * i) for i in range(2)]
    x1_sb = [man(f"x1_{i}", [128, DM], F32, C0 + 24576 + 4096 * i) for i in range(2)]
    ac_bf = man("ac_bf", [128, DM], BF16, C0 + 32768)
    aT_bf = man("aT_bf", [128, DM], BF16, C0 + 34816)
    wg_sb = [man("wg_p0", [128, 8 * 512], BF16, CW),
             man("wg_p1", [128, 8 * 512], BF16, C0)]
    wp_sb = [man("wp_p0", [128, 8 * 512], BF16, CW + 8192),
             man("wp_p1", [128, 8 * 512], BF16, C0 + 8192)]
    wo_sb = [man("wo_p0", [128, 4 * DM], BF16, CW + 16384),
             man("wo_p1", [128, 4 * DM], BF16, C0 + 16384)]
    t_sb = [man(f"t3_{i}", [128, 4 * 512], BF16, C0 + 24576 + 4096 * i) for i in range(2)]
    rt_sb = [man(f"rt3_{i}", [128, 512], F32, C0 + 32768 + 2048 * i) for i in range(2)]
    assert nc.sbuf_base <= C0, f"bottom region overflow: {nc.sbuf_base} > {C0}"

    esp = ExitStack()
    psf = lambda name, shape, dt=F32: esp.enter_context(nc.psum_tensor(name, shape, dt))
    p1q = [psf(f"p1q{i}", [128, 512]) for i in range(2)]
    p1kv = [psf(f"p1kv{i}", [128, 512]) for i in range(2)]
    p1kvx = [psf(f"p1kvx{i}", [128, 4 * KVW]) for i in range(2)]
    p1tr = [psf(f"p1tr{i}", [128, 128], BF16) for i in range(2)]
    esp.close()
    esp = ExitStack()
    psf = lambda name, shape, dt=F32: esp.enter_context(nc.psum_tensor(name, shape, dt))
    p2at = [psf(f"p2at{i}", [128, 260]) for i in range(2)]
    p2pr = [psf(f"p2pr{i}", [128, 512]) for i in range(2)]
    p2tr = [psf(f"p2tr{i}", [128, 128], BF16) for i in range(2)]
    p2sc = psf("p2sc", [128, 8])
    p2bc = psf("p2bc", [128, 1])
    esp.close()
    esp = ExitStack()
    psf = lambda name, shape, dt=F32: esp.enter_context(nc.psum_tensor(name, shape, dt))
    psk1 = psf("psk1", [8, 1])
    psk2 = psf("psk2", [128, 8])
    esp.close()
    esp = ExitStack()
    psf = lambda name, shape, dt=F32: esp.enter_context(nc.psum_tensor(name, shape, dt))
    p3g = [psf(f"p3g{i}", [128, 512]) for i in range(2)]
    p3p = [psf(f"p3p{i}", [128, 512]) for i in range(2)]
    p3o = [psf(f"p3o{i}", [128, 512]) for i in range(4)]
    esp.close()

    SM = {}
    for nm in ["init", "wqkv", "wproj", "rw", "brow", "x0", "x1p", "x2a", "x2b",
               "xf", "lnA", "lnB", "h1", "tp", "tpf", "q", "qf", "kv", "kvf",
               "kbf", "kvb", "at", "atf", "ac", "pr", "prf", "b2",
               "bst", "bst1", "h2T", "sc", "scf", "skv", "sks", "skt", "sktf",
               "w3a", "w3b", "w3f", "gp", "gpf", "rl", "t3", "t3f", "o3",
               "o3f", "bi", "st", "cst", "vv", "ss", "tpd", "tpd1", "trad", "trhd"]:
        SM[nm] = Sem(nc, es, "s_" + nm)
    sx = [SM["x0"], SM["x1p"]]
    sx2 = [SM["x2a"], SM["x2b"]]
    sbst = [SM["bst"], SM["bst1"]]
    sw3 = [SM["w3a"], SM["w3b"]]

    stpd = None
    P = Prog()
    W = P.wait
    EM = P.emit

    def inc(sem, n=1):
        sem.v += n
        return sem.v

    def dma(eng, out, in_, sem, slow=False, tr=False):
        EM(eng, lambda e, o=out, i=in_, h=sem.h, sl=slow, tr=tr: e.dma_start(
            out=o, in_=i, allow_slow_non_contiguous=sl, transpose=tr).then_inc(h, 16))
        inc(sem, 16)

    def vsem(fn):
        EM(V, lambda v: fn(v).then_inc(SM["vv"].h, 1))
        inc(SM["vv"])
        W(V, SM["vv"])

    def ssem(fn):
        EM(S, lambda s: fn(s).then_inc(SM["ss"].h, 1))
        inc(SM["ss"])
        W(S, SM["ss"])

    dma(SY, identb[:], identb_d, SM["init"])
    dma(SY, tmp1[0:1, 0:1], temp_d.unsqueeze(0), SM["init"])
    dma(GP, wqkv_sb[:].rearrange("p (k c) -> p k c", c=3072),
        wqkv_d.rearrange("(k p) c -> p k c", p=128), SM["wqkv"])
    dma(GP, bqkv_row[:], bqkv_d.unsqueeze(0), SM["brow"])
    dma(GP, bproj_row[:], bproj_d.unsqueeze(0), SM["brow"])
    dma(GP, rb_row[:], rb_d.unsqueeze(0), SM["brow"])
    dma(GP, rws[:], rw_d, SM["rw"])

    def wchunk(e, c, b):
        dma(GP, wg_sb[b][:].rearrange("p (k f) -> p k f", f=512),
            wg_d[e].rearrange("(k p) f -> p k f", p=128)[:, :, 512 * c:512 * (c + 1)], sw3[b])
        dma(GP, wp_sb[b][:].rearrange("p (k f) -> p k f", f=512),
            wp_d[e].rearrange("(k p) f -> p k f", p=128)[:, :, 512 * c:512 * (c + 1)], sw3[b])
        dma(GP, wo_sb[b][:].rearrange("p (f d) -> p f d", d=DM),
            wo_d[e].rearrange("(f p) d -> p f d", p=128)[:, 4 * c:4 * (c + 1), :], sw3[b])
    wchunk(0, 0, 0)

    EM(V, lambda v: v.memset(onesrow[:], 1.0))
    EM(V, lambda v: v.memset(onesf1[:], 1.0))
    EM(V, lambda v: v.memset(onesb[:], 1.0))
    EM(V, lambda v: v.memset(onesf8[:], 1.0))
    EM(V, lambda v: v.memset(onescol[:], 1.0))
    EM(V, lambda v: v.memset(p1kvx[0][:], 0.0))
    EM(V, lambda v: v.memset(p1kvx[1][:], 0.0))
    W(V, SM["init"])
    EM(V, lambda v: v.tensor_copy(eye8[:], identb[0:8, 0:8]).then_inc(SM["cst"].h, 1))
    inc(SM["cst"])
    W(T, SM["init"])
    W(T, SM["cst"])
    EM(T, lambda t: t.matmul(p2bc[:], onesf1[:], tmp1[0:1, 0:1], start=True, stop=True).then_inc(SM["skt"].h, 1))
    inc(SM["skt"])
    W(V, SM["skt"])
    vsem(lambda v: v.tensor_copy(rtA[:, 0:1], p2bc[:]))
    vsem(lambda v: v.tensor_scalar(rtA[:, 1:2], rtA[:, 0:1], 0.1, None, ALU.max))
    EM(V, lambda v: v.reciprocal(rtemp[:], rtA[:, 1:2]).then_inc(SM["sktf"].h, 1))
    inc(SM["sktf"])
    W(T, SM["rw"])
    W(T, SM["sktf"])
    for k in range(8):
        if k >= 2:
            W(T, SM["sktf"], SM["sktf"].v - 1)
        EM(T, lambda t, k=k: t.transpose(p1tr[k % 2][:, 0:8], rws[:, 128 * k:128 * (k + 1)],
                                         identb[0:8, 0:8]).then_inc(SM["skt"].h, 1))
        inc(SM["skt"])
        W(V, SM["skt"])
        EM(V, lambda v, k=k: v.tensor_copy(rwT[:, 8 * k:8 * (k + 1)], p1tr[k % 2][:, 0:8]).then_inc(SM["sktf"].h, 1))
        inc(SM["sktf"])

    stpd = [SM["tpd"], SM["tpd1"]]
    tpd_hist = []
    qf_hist = []
    kvf_hist = []
    lnpar = 8

    W(T, SM["wqkv"])
    W(T, SM["brow"], 48)
    W(T, SM["sktf"])

    def p1_ln_v(i):
        pr = i % 2
        if i >= 2:
            W(SY, SM["h1"], i - 1)
        dma(SY, xt_sb[pr][:], x_d.rearrange("(s p) d -> s p d", p=128)[i], sx[pr])
        W(V, sx[pr])
        if i >= 2:
            W(V, SM["h1"], i - 1)
        c0 = lnpar * pr
        m0 = 12 * pr
        EM(V, lambda v, pr=pr, m0=m0: v.bn_stats(mv12[:, m0:m0 + 6], xt_sb[pr][:, 0:512]))
        vsem(lambda v, pr=pr, m0=m0: v.bn_stats(mv12[:, m0 + 6:m0 + 12], xt_sb[pr][:, 512:1024]))
        vsem(lambda v, pr=pr, m0=m0: v.bn_aggr(mv2[:, 2 * pr:2 * pr + 2], mv12[:, m0:m0 + 12]))
        EM(V, lambda v, pr=pr, c0=c0: v.tensor_scalar(lnc[:, c0:c0 + 1], mv2[:, 2 * pr + 1:2 * pr + 2],
                                                      EPS_LN, None, ALU.add).then_inc(SM["lnA"].h, 1))
        inc(SM["lnA"])
        W(S, SM["lnA"])
        ssem(lambda s, c0=c0: s.activation(lnc[:, c0 + 1:c0 + 2], lnc[:, c0:c0 + 1], AF.Ln))
        EM(S, lambda s, c0=c0: s.activation(lnc[:, c0 + 2:c0 + 3], lnc[:, c0 + 1:c0 + 2],
                                            AF.Exp, scale=-0.5).then_inc(SM["lnB"].h, 1))
        inc(SM["lnB"])
        W(S, SM["lnB"])
        W(V, SM["lnB"])
        EM(V, lambda v, c0=c0, pr=pr: v.scalar_tensor_tensor(lnc[:, c0 + 3:c0 + 4], mv2[:, 2 * pr:2 * pr + 1],
                                                             -1.0, lnc[:, c0 + 2:c0 + 3],
                                                             ALU.mult, ALU.mult).then_inc(SM["lnA"].h, 1))
        inc(SM["lnA"])

    def p1_ln_s(i):
        pr = i % 2
        c0 = lnpar * pr
        W(S, SM["lnA"])
        if i >= 2:
            W(S, stpd[tpd_hist[i - 2][0]], tpd_hist[i - 2][1])
        EM(S, lambda s, pr=pr, c0=c0, i=i: s.activation(h1b[i % 2][:], xt_sb[pr][:], AF.Identity,
                                                   bias=lnc[:, c0 + 3:c0 + 4],
                                                   scale=lnc[:, c0 + 2:c0 + 3]).then_inc(SM["h1"].h, 1))
        inc(SM["h1"])

    def p1_tr(i):
        sup, s4 = divmod(i, 4)
        W(SY, SM["h1"], i + 1)
        if sup >= 1:
            W(SY, SM["q"], 8 * sup)
        for j in range(8):
            dma(SY, h1T[:, 512 * j + 128 * s4:512 * j + 128 * (s4 + 1)],
                h1b[i % 2][:, 128 * j:128 * (j + 1)], stpd[i % 2], tr=True)
        tpd_hist.append((i % 2, stpd[i % 2].v))

    def p1_kvkvx(i):
        sup, s4 = divmod(i, 4)
        pr = i % 2
        W(T, stpd[tpd_hist[i][0]], tpd_hist[i][1])
        for cc in range(4):
            kvi = i * 4 + cc
            if kvi >= 2:
                W(T, SM["kvf"], kvf_hist[kvi - 2])
            pb = p1kv[kvi % 2]
            for k in range(8):
                EM(T, lambda t, pb=pb, k=k, s4=s4, cc=cc: t.matmul(
                    pb[:], h1T[:, 512 * k + 128 * s4:512 * k + 128 * (s4 + 1)],
                    wqkv_sb[:, 3072 * k + 1024 + 512 * cc:3072 * k + 1024 + 512 * (cc + 1)],
                    start=(k == 0), stop=False))
            EM(T, lambda t, pb=pb, cc=cc: t.matmul(
                pb[:], onesrow[0:1, 0:128],
                bqkv_row[0:1, 1024 + 512 * cc:1024 + 512 * (cc + 1)],
                start=False, stop=True).then_inc(SM["kv"].h, 1))
            inc(SM["kv"])
            if cc < 2:
                W(V, SM["kv"])
                W(V, SM["sks"])
                EM(V, lambda v, pb=pb, cc=cc: v.tensor_single_scalar(
                    qtmp1[:] if cc == 0 else qtmp2[:], pb[:], 0.0, ALU.min).then_inc(SM["skv"].h, 1))
                inc(SM["skv"])
                W(S, SM["skv"])
                W(S, SM["kvf"])
                W(S, SM["qf"])
                EM(S, lambda s, cc=cc: s.activation(qtmp3[:],
                                                    qtmp1[:] if cc == 0 else qtmp2[:],
                                                    AF.Exp).then_inc(SM["sks"].h, 1))
                inc(SM["sks"])
                W(V, SM["sks"])
                EM(V, lambda v, pb=pb, i=i, cc=cc: v.scalar_tensor_tensor(
                    kb_bf[i % 2][:, 512 * cc:512 * (cc + 1)], pb[:], 0.0, qtmp3[:],
                    ALU.max, ALU.add).then_inc(SM["kvf"].h, 1))
            else:
                W(V, SM["kv"])
                EM(V, lambda v, pb=pb, i=i, cc=cc: v.tensor_copy(
                    vb_bf[i % 2][:, 512 * (cc - 2):512 * (cc - 1)], pb[:]).then_inc(SM["kvf"].h, 1))
            kvf_hist.append(inc(SM["kvf"]))
        W(T, SM["kvf"])
        for pair in range(8):
            g, jj = divmod(pair, 4)
            for half in range(2):
                hd0 = 128 * pair + 64 * half
                last = pair == 7 and half == 1
                EM(T, lambda t, g=g, jj=jj, half=half, hd0=hd0, i=i: t.matmul(
                    p1kvx[g][64 * half:64 * (half + 1), KVW * jj:KVW * jj + 64],
                    kb_bf[i % 2][:, hd0:hd0 + 64], vb_bf[i % 2][:, hd0:hd0 + 64],
                    start=False, stop=False, skip_group_check=True))
                mmfn = lambda t, g=g, jj=jj, half=half, hd0=hd0, i=i: t.matmul(
                    p1kvx[g][64 * half:64 * (half + 1), KVW * jj + 64:KVW * jj + 65],
                    kb_bf[i % 2][:, hd0:hd0 + 64], onesb[:],
                    start=False, stop=False, skip_group_check=True)
                if last:
                    EM(T, lambda t, f=mmfn: f(t).then_inc(SM["kbf"].h, 1))
                else:
                    EM(T, mmfn)
        inc(SM["kbf"])

    def p1_q(sup):
        for qc in range(8):
            qi = sup * 8 + qc
            if qi >= 2:
                W(T, SM["qf"], qf_hist[qi - 2])
            for k in range(8):
                EM(T, lambda t, qc=qc, k=k, qi=qi: t.matmul(
                    p1q[qi % 2][:], wqkv_sb[:, 3072 * k + 128 * qc:3072 * k + 128 * (qc + 1)],
                    h1T[:, 512 * k:512 * (k + 1)], start=(k == 0), stop=False))
            EM(T, lambda t, qc=qc, qi=qi: t.matmul(
                p1q[qi % 2][:], bqkv_row[0:1, 128 * qc:128 * (qc + 1)],
                onesrow[0:1, 0:512], start=False, stop=True).then_inc(SM["q"].h, 1))
            inc(SM["q"])
            W(V, SM["q"])
            W(V, SM["sks"])
            EM(V, lambda v, qi=qi, qc=qc: v.tensor_single_scalar(
                qtmp1[:] if qc % 2 == 0 else qtmp2[:], p1q[qi % 2][:], 0.0,
                ALU.min).then_inc(SM["skv"].h, 1))
            inc(SM["skv"])
            W(S, SM["skv"])
            W(S, SM["kvf"])
            W(S, SM["qf"])
            EM(S, lambda s, qc=qc: s.activation(qtmp3[:],
                                                qtmp1[:] if qc % 2 == 0 else qtmp2[:],
                                                AF.Exp).then_inc(SM["sks"].h, 1))
            inc(SM["sks"])
            W(V, SM["sks"])
            EM(V, lambda v, qc=qc, sup=sup, qi=qi: v.scalar_tensor_tensor(
                q_sb[:, NTOK * qc + 512 * sup:NTOK * qc + 512 * (sup + 1)],
                p1q[qi % 2][:], 0.0, qtmp3[:], ALU.max, ALU.add).then_inc(SM["qf"].h, 1))
            qf_hist.append(inc(SM["qf"]))

    p1_ln_v(0)
    p1_ln_s(0)
    p1_ln_v(1)
    p1_ln_s(1)
    for i in range(NSUB):
        if i + 2 < NSUB:
            p1_ln_v(i + 2)
        p1_tr(i)
        if i + 2 < NSUB:
            p1_ln_s(i + 2)
        p1_kvkvx(i)
        if i % 4 == 3:
            p1_q(i // 4)

    W(V, SM["kbf"])
    EM(V, lambda v: v.memset(kvbd[:], 0.0).then_inc(SM["skv"].h, 1))
    inc(SM["skv"])
    W(V, SM["skv"])
    for pair in range(8):
        g, jj = divmod(pair, 4)
        EM(V, lambda v, pair=pair, g=g, jj=jj: v.tensor_copy(
            kvbd[0:64, 130 * pair:130 * pair + 64], p1kvx[g][0:64, KVW * jj:KVW * jj + 64]))
        EM(V, lambda v, pair=pair, g=g, jj=jj: v.tensor_copy(
            kvbd[0:64, 130 * pair + 64:130 * pair + 65], p1kvx[g][0:64, KVW * jj + 64:KVW * jj + 65]))
        EM(V, lambda v, pair=pair, g=g, jj=jj: v.tensor_copy(
            kvbd[64:128, 130 * pair + 65:130 * pair + 129], p1kvx[g][64:128, KVW * jj:KVW * jj + 64]))
        if pair < 7:
            EM(V, lambda v, pair=pair, g=g, jj=jj: v.tensor_copy(
                kvbd[64:128, 130 * pair + 129:130 * pair + 130], p1kvx[g][64:128, KVW * jj + 64:KVW * jj + 65]))
        else:
            EM(V, lambda v, pair=pair, g=g, jj=jj: v.tensor_copy(
                kvbd[64:128, 130 * pair + 129:130 * pair + 130],
                p1kvx[g][64:128, KVW * jj + 64:KVW * jj + 65]).then_inc(SM["kvb"].h, 1))
    inc(SM["kvb"])

    W(GP, SM["qf"], qf_hist[-1])
    W(GP, SM["kbf"])
    dma(GP, wproj_sb[:].rearrange("p (k c) -> p k c", c=DM),
        wproj_d.rearrange("(k p) c -> p k c", p=128), SM["wproj"])

    atf_hist = []
    prf_hist = []
    scf_hist = []
    b2_hist = []
    bst_hist = []
    trad_hist = []
    trhd_hist = []
    p2tpf_hist = []
    p2seq = [0]

    W(T, SM["kvb"])
    W(T, SM["wproj"])

    def p2_transpose_grp(srcbuf, dst_slices):
        for j in range(8):
            sq = p2seq[0]
            p2seq[0] += 1
            if sq >= 2:
                W(T, SM["tpf"], p2tpf_hist[sq - 2])
            EM(T, lambda t, j=j, sq=sq, srcbuf=srcbuf: t.transpose(
                p2tr[sq % 2][:], srcbuf[:, 128 * j:128 * (j + 1)], identb[:]).then_inc(SM["tp"].h, 1))
            inc(SM["tp"])
            W(V, SM["tp"])
            EM(V, lambda v, j=j, sq=sq, d=dst_slices[j]: v.tensor_copy(
                d, p2tr[sq % 2][:]).then_inc(SM["tpf"].h, 1))
            p2tpf_hist.append(inc(SM["tpf"]))

    def p2_at(i):
        pr = i % 2
        if i < 2:
            W(SY, SM["qf"], qf_hist[-1])
            W(SY, SM["kbf"])
        else:
            W(SY, sbst[pr], bst_hist[i - 2][pr])
        dma(SY, xt2_sb[pr][:], x_d.rearrange("(s p) d -> s p d", p=128)[i], sx2[pr])
        for g4 in range(4):
            agi = i * 4 + g4
            if agi >= 2:
                W(T, SM["atf"], atf_hist[agi - 2])
            for u in range(2):
                pair = g4 * 2 + u
                mmfn = lambda t, g4=g4, u=u, pair=pair, i=i: t.matmul(
                    p2at[g4 % 2][:, 130 * u:130 * (u + 1)],
                    q_sb[:, NTOK * pair + 128 * i:NTOK * pair + 128 * (i + 1)],
                    kvbd[:, 130 * pair:130 * (pair + 1)], start=True, stop=True)
                if u == 1:
                    EM(T, lambda t, f=mmfn: f(t).then_inc(SM["at"].h, 1))
                else:
                    EM(T, mmfn)
            inc(SM["at"])
            W(V, SM["at"])
            if g4 == 0 and i >= 1:
                W(V, SM["tpf"], trad_hist[i - 1])
            pa = p2at[g4 % 2]
            pav = pa[:].rearrange("p (u h c) -> p u h c", h=2, c=65)
            vsem(lambda v, pav=pav, g4=g4: v.tensor_scalar(
                z16b[:, 4 * g4:4 * g4 + 4].rearrange("p (u h) -> p u h", h=2).unsqueeze(3),
                pav[:, :, :, 64:65], 1e-8, None, ALU.add))
            vsem(lambda v, g4=g4: v.reciprocal(z16c[:, 4 * g4:4 * g4 + 4], z16b[:, 4 * g4:4 * g4 + 4]))
            EM(V, lambda v, pav=pav, g4=g4: v.tensor_tensor(
                ac_bf[:, 256 * g4:256 * (g4 + 1)].rearrange("p (u h d) -> p u h d", h=2, d=64),
                pav[:, :, :, 0:64],
                z16c[:, 4 * g4:4 * g4 + 4].rearrange("p (u h) -> p u h", h=2).unsqueeze(3).broadcast_to([128, 2, 2, 64]),
                ALU.mult).then_inc(SM["atf"].h, 1))
            atf_hist.append(inc(SM["atf"]))

    def p2_tra(i):
        W(T, SM["atf"], atf_hist[i * 4 + 3])
        p2_transpose_grp(ac_bf, [aT_bf[:, 128 * j:128 * (j + 1)] for j in range(8)])
        trad_hist.append(SM["tpf"].v)

    def p2_pj(i):
        pr = i % 2
        c0 = lnpar * pr
        m0 = 12 * pr
        W(T, SM["tpf"], trad_hist[i])
        for dmh in range(2):
            pri = i * 2 + dmh
            if pri >= 2:
                W(T, SM["prf"], prf_hist[pri - 2])
            for k in range(8):
                EM(T, lambda t, k=k, dmh=dmh, pri=pri: t.matmul(
                    p2pr[pri % 2][:], aT_bf[:, 128 * k:128 * (k + 1)],
                    wproj_sb[:, DM * k + 512 * dmh:DM * k + 512 * (dmh + 1)],
                    start=(k == 0), stop=False))
            EM(T, lambda t, dmh=dmh, pri=pri: t.matmul(
                p2pr[pri % 2][:], onesrow[0:1, 0:128],
                bproj_row[0:1, 512 * dmh:512 * (dmh + 1)], start=False, stop=True).then_inc(SM["pr"].h, 1))
            inc(SM["pr"])
            W(V, SM["pr"])
            W(V, sx2[pr])
            if i >= 2:
                W(V, SM["b2"], b2_hist[i - 2])
            EM(V, lambda v, pr=pr, dmh=dmh, pri=pri: v.scalar_tensor_tensor(
                x1_sb[pr][:, 512 * dmh:512 * (dmh + 1)], p2pr[pri % 2][:], 1.0,
                xt2_sb[pr][:, 512 * dmh:512 * (dmh + 1)], ALU.mult, ALU.add).then_inc(SM["prf"].h, 1))
            prf_hist.append(inc(SM["prf"]))
        W(V, SM["prf"])
        EM(V, lambda v, pr=pr, m0=m0: v.bn_stats(mv12[:, m0:m0 + 6], x1_sb[pr][:, 0:512]))
        vsem(lambda v, pr=pr, m0=m0: v.bn_stats(mv12[:, m0 + 6:m0 + 12], x1_sb[pr][:, 512:1024]))
        vsem(lambda v, pr=pr, m0=m0: v.bn_aggr(mv2[:, 2 * pr:2 * pr + 2], mv12[:, m0:m0 + 12]))
        EM(V, lambda v, pr=pr, c0=c0: v.tensor_scalar(lnc[:, c0:c0 + 1], mv2[:, 2 * pr + 1:2 * pr + 2],
                                                      EPS_LN, None, ALU.add).then_inc(SM["lnA"].h, 1))
        inc(SM["lnA"])
        W(S, SM["lnA"])
        ssem(lambda s, c0=c0: s.activation(lnc[:, c0 + 1:c0 + 2], lnc[:, c0:c0 + 1], AF.Ln))
        EM(S, lambda s, c0=c0: s.activation(lnc[:, c0 + 2:c0 + 3], lnc[:, c0 + 1:c0 + 2],
                                            AF.Exp, scale=-0.5).then_inc(SM["lnB"].h, 1))
        inc(SM["lnB"])
        W(S, SM["lnB"])
        W(V, SM["lnB"])
        EM(V, lambda v, c0=c0, pr=pr: v.scalar_tensor_tensor(lnc[:, c0 + 3:c0 + 4], mv2[:, 2 * pr:2 * pr + 1],
                                                             -1.0, lnc[:, c0 + 2:c0 + 3],
                                                             ALU.mult, ALU.mult))
        EM(V, lambda v, c0=c0: v.tensor_scalar(lnc[:, c0 + 4:c0 + 5], lnc[:, c0 + 2:c0 + 3],
                                               1.0, None, ALU.add).then_inc(SM["lnA"].h, 1))
        inc(SM["lnA"])
        W(S, SM["lnA"])
        if i >= 1:
            W(S, SM["tpf"], trhd_hist[i - 1])
        EM(S, lambda s, pr=pr, c0=c0: s.activation(acH[:], x1_sb[pr][:], AF.Identity,
                                                   bias=lnc[:, c0 + 3:c0 + 4],
                                                   scale=lnc[:, c0 + 2:c0 + 3]))
        EM(S, lambda s, pr=pr, c0=c0: s.activation(xt2_sb[pr][:], x1_sb[pr][:], AF.Identity,
                                                   bias=lnc[:, c0 + 3:c0 + 4],
                                                   scale=lnc[:, c0 + 4:c0 + 5]).then_inc(SM["b2"].h, 1))
        b2_hist.append(inc(SM["b2"]))
        W(SY, SM["b2"])
        dma(SY, base_d.rearrange("(s p) d -> s p d", p=128)[i], xt2_sb[pr][:], sbst[pr])
        bst_hist.append((SM["bst"].v, SM["bst1"].v))

    def p2_trh(i):
        W(T, SM["b2"], b2_hist[i])
        p2_transpose_grp(acH, [h2T[:, NTOK * j + 128 * i:NTOK * j + 128 * (i + 1)] for j in range(8)])
        trhd_hist.append(SM["tpf"].v)

    def p2_rt(i):
        W(T, SM["tpf"], trhd_hist[i])
        if i >= 1:
            W(T, SM["scf"], scf_hist[i - 1])
        for k in range(8):
            EM(T, lambda t, k=k, i=i: t.matmul(p2sc[:], h2T[:, NTOK * k + 128 * i:NTOK * k + 128 * (i + 1)],
                                               rwT[:, 8 * k:8 * (k + 1)], start=(k == 0), stop=False))
        EM(T, lambda t: t.matmul(p2sc[:], onesrow[0:1, 0:128], rb_row[:],
                                 start=False, stop=True).then_inc(SM["sc"].h, 1))
        inc(SM["sc"])
        W(V, SM["sc"])
        EM(V, lambda v, i=i: v.tensor_scalar(scores[:, 8 * i:8 * (i + 1)], p2sc[:],
                                             rtemp[:, 0:1], None, ALU.mult).then_inc(SM["scf"].h, 1))
        scf_hist.append(inc(SM["scf"]))

    for i in range(NSUB):
        if i >= 1:
            p2_tra(i - 1)
        if i >= 2:
            p2_trh(i - 2)
            p2_rt(i - 2)
        p2_at(i)
        if i >= 1:
            p2_pj(i - 1)
    p2_tra(NSUB - 1)
    p2_trh(NSUB - 2)
    p2_rt(NSUB - 2)
    p2_pj(NSUB - 1)
    p2_trh(NSUB - 1)
    p2_rt(NSUB - 1)

    r3 = lambda buf: buf[:].rearrange("p (s e) -> p s e", e=8)
    reE = lambda small: small[:].unsqueeze(2).broadcast_to([128, 32, 8])
    reS = lambda small8: small8[:].unsqueeze(1).broadcast_to([128, 32, 8])

    def v2s(fn):
        EM(V, lambda v: fn(v).then_inc(SM["skv"].h, 1))
        inc(SM["skv"])
        W(S, SM["skv"])

    def s2v(fn):
        EM(S, lambda s: fn(s).then_inc(SM["sks"].h, 1))
        inc(SM["sks"])
        W(V, SM["sks"])

    W(V, SM["scf"], scf_hist[-1])
    for it in range(3):
        vsem(lambda v: v.reduce_max(sk32[:], r3(scores), axis=AX.X))
        v2s(lambda v: v.tensor_tensor(r3(skE), r3(scores), reE(sk32), ALU.subtract))
        s2v(lambda s: s.activation(sk2[:], skE[:], AF.Exp))
        v2s(lambda v: v.reduce_sum(sk32b[:], r3(sk2), axis=AX.X))
        s2v(lambda s: s.activation(sk32c[:], sk32b[:], AF.Ln))
        v2s(lambda v: v.tensor_tensor(r3(sk2), r3(skE), reE(sk32c), ALU.subtract))
        s2v(lambda s: s.activation(wsel[:], sk2[:], AF.Exp))
        EM(V, lambda v: v.reduce_sum(sk8[:], wsel[:].rearrange("p (s e) -> p e s", e=8),
                                     axis=AX.X).then_inc(SM["skv"].h, 1))
        inc(SM["skv"])
        W(T, SM["skv"])
        if it >= 1:
            W(T, SM["sktf"])
        EM(T, lambda t: t.matmul(psk1[:], sk8[:], onescol[:], start=True, stop=True).then_inc(SM["skt"].h, 1))
        inc(SM["skt"])
        W(V, SM["skt"])
        v2s(lambda v: v.tensor_copy(skL8[:], psk1[:]))
        s2v(lambda s: s.activation(skL8b[:], skL8[:], AF.Ln))
        EM(V, lambda v: v.tensor_tensor(skdg[:], eye8[:], skL8b[:, 0:1].broadcast_to([8, 8]),
                                        ALU.mult).then_inc(SM["skv"].h, 1))
        inc(SM["skv"])
        W(T, SM["skv"])
        EM(T, lambda t: t.matmul(psk2[:], onesf8[:], skdg[:], start=True, stop=True).then_inc(SM["skt"].h, 1))
        inc(SM["skt"])
        W(V, SM["skt"])
        EM(V, lambda v: v.tensor_copy(sk8[:], psk2[:]).then_inc(SM["sktf"].h, 1))
        inc(SM["sktf"])
        W(V, SM["sktf"])
        EM(V, lambda v: v.tensor_tensor(r3(scores), r3(sk2), reS(sk8), ALU.subtract).then_inc(SM["skv"].h, 1))
        inc(SM["skv"])
        W(S, SM["skv"])
        W(V, SM["skv"])
    s2v(lambda s: s.activation(skE[:], scores[:], AF.Exp))
    vsem(lambda v: v.reduce_sum(sk32[:], r3(skE), axis=AX.X))
    vsem(lambda v: v.tensor_scalar(sk32b[:], sk32[:], 1e-8, None, ALU.add))
    vsem(lambda v: v.reciprocal(sk32c[:], sk32b[:]))
    vsem(lambda v: v.tensor_tensor(r3(sk2), r3(skE), reE(sk32c), ALU.mult))
    vsem(lambda v: v.reduce_max(sk32[:], r3(sk2), axis=AX.X))
    vsem(lambda v: v.tensor_tensor(r3(wsel), r3(sk2), reE(sk32), ALU.is_ge))
    vsem(lambda v: v.scalar_tensor_tensor(r3(skE), r3(wsel), -1e30, r3(sk2),
                                          ALU.mult, ALU.add))
    vsem(lambda v: v.reduce_max(sk32b[:], r3(skE), axis=AX.X))
    vsem(lambda v: v.tensor_tensor(sk32d[:], sk32[:], sk32b[:], ALU.add))
    vsem(lambda v: v.tensor_scalar(sk32[:], sk32d[:], 1e-8, None, ALU.add))
    vsem(lambda v: v.reciprocal(sk32d[:], sk32[:]))
    vsem(lambda v: v.tensor_tensor(r3(wsel), r3(sk2), reE(sk32b), ALU.is_ge))
    vsem(lambda v: v.tensor_tensor(r3(skE), r3(sk2), r3(wsel), ALU.mult))
    EM(V, lambda v: v.tensor_tensor(r3(wsel), r3(skE), reE(sk32d),
                                    ALU.mult).then_inc(SM["skv"].h, 1))
    inc(SM["skv"])
    wsel_done = SM["skv"].v

    h2T_done = trhd_hist[-1]
    su_done = {}
    gpf_hist = []
    pair_o3f = []
    chunk_o3f = []
    o3f_hist = []
    w3_hist = {0: [], 1: []}
    w3_hist[0].append(sw3[0].v)

    W(SY, SM["scf"], scf_hist[-1])
    W(SY, SM["bst"], bst_hist[-1][0])
    W(SY, SM["bst1"], bst_hist[-1][1])
    for su in range(16):
        dma(SY, out_acc[:, DM * su:DM * (su + 1)],
            base_d.rearrange("(s p) d -> s p d", p=128)[su], SM["bi"])
    SY_DEFERRED = True

    W(T, SM["tpf"], h2T_done)
    W(T, SM["sktf"])
    first_evac_gate_done = [False] * NW

    def emitA(p):
        ci, q = divmod(p, 4)
        w, rem = divmod(ci, E * NCF)
        e, c = divmod(rem, NCF)
        b = ci % 2
        qt = q % 2
        if q == 0:
            W(T, sw3[b], w3_hist[b][ci // 2])
        if q == 1:
            nxt = ci + 1
            if nxt < NW * E * NCF:
                nw_, nrem = divmod(nxt, E * NCF)
                ne_, nc_ = divmod(nrem, NCF)
                nb = nxt % 2
                if nxt == 1:
                    W(GP, SM["bst"], bst_hist[-1][0])
                    W(GP, SM["bst1"], bst_hist[-1][1])
                    W(GP, SM["tpf"], h2T_done)
                    W(GP, SM["scf"], scf_hist[-1])
                else:
                    W(GP, SM["o3f"], chunk_o3f[nxt - 2])
                wchunk(ne_, nc_, nb)
                w3_hist[nb].append(sw3[nb].v)
        for fsub in range(4):
            it3 = ci * 16 + q * 4 + fsub
            if it3 >= 2:
                W(T, SM["gpf"], gpf_hist[it3 - 2])
            for k in range(8):
                EM(T, lambda t, b=b, k=k, fsub=fsub, w=w, q=q, it3=it3: t.matmul(
                    p3g[it3 % 2][:], wg_sb[b][:, 512 * k + 128 * fsub:512 * k + 128 * (fsub + 1)],
                    h2T[:, NTOK * k + WAVE * w + 512 * q:NTOK * k + WAVE * w + 512 * (q + 1)],
                    start=(k == 0), stop=(k == 7)))
            for k in range(8):
                mmfn = lambda t, b=b, k=k, fsub=fsub, w=w, q=q, it3=it3: t.matmul(
                    p3p[it3 % 2][:], wp_sb[b][:, 512 * k + 128 * fsub:512 * k + 128 * (fsub + 1)],
                    h2T[:, NTOK * k + WAVE * w + 512 * q:NTOK * k + WAVE * w + 512 * (q + 1)],
                    start=(k == 0), stop=(k == 7))
                if k == 7:
                    EM(T, lambda t, f=mmfn: f(t).then_inc(SM["gp"].h, 1))
                else:
                    EM(T, mmfn)
            inc(SM["gp"])
            W(S, SM["gp"])
            if it3 >= 2:
                W(S, SM["gpf"], gpf_hist[it3 - 2])
            EM(S, lambda s, it3=it3: s.activation(rt_sb[it3 % 2][:], p3g[it3 % 2][:],
                                                  AF.Relu).then_inc(SM["rl"].h, 1))
            inc(SM["rl"])
            W(V, SM["rl"])
            if p == 0 and fsub == 0:
                W(V, SM["bst"], bst_hist[-1][0])
                W(V, SM["bst1"], bst_hist[-1][1])
                W(V, SM["skv"], wsel_done)
            if p >= 2:
                W(V, SM["o3f"], pair_o3f[p - 2])
            EM(V, lambda v, qt=qt, fsub=fsub, it3=it3: v.tensor_tensor(
                t_sb[qt][:, 512 * fsub:512 * (fsub + 1)], rt_sb[it3 % 2][:],
                p3p[it3 % 2][:], ALU.mult).then_inc(SM["gpf"].h, 2))
            gpf_hist.append(inc(SM["gpf"], 2))

    def emitB(p):
        ci, q = divmod(p, 4)
        w, rem = divmod(ci, E * NCF)
        e, c = divmod(rem, NCF)
        b = ci % 2
        qt = q % 2
        W(T, SM["gpf"], gpf_hist[ci * 16 + q * 4 + 3])
        for dmh in range(2):
            og = ci * 8 + q * 2 + dmh
            if og >= 1:
                W(T, SM["o3f"], o3f_hist[og - 1])
            inc(SM["o3"])
            for s4 in range(4):
                for fsub in range(4):
                    mmfn = lambda t, qt=qt, s4=s4, fsub=fsub, b=b, dmh=dmh: t.matmul(
                        p3o[s4][:], t_sb[qt][:, 512 * fsub + 128 * s4:512 * fsub + 128 * (s4 + 1)],
                        wo_sb[b][:, DM * fsub + 512 * dmh:DM * fsub + 512 * (dmh + 1)],
                        start=(fsub == 0), stop=(fsub == 3))
                    if fsub == 3 and s4 == 3:
                        EM(T, lambda t, f=mmfn: f(t).then_inc(SM["o3"].h, 1))
                    else:
                        EM(T, mmfn)
                W(V, SM["o3"])
                if not first_evac_gate_done[w]:
                    W(V, SM["bi"], 256 * (w + 1))
                    first_evac_gate_done[w] = True
                su = q * 4 + s4
                col = (16 * w + su) * 8 + e
                EM(V, lambda v, s4=s4, su=su, col=col, dmh=dmh: v.scalar_tensor_tensor(
                    out_acc[:, DM * su + 512 * dmh:DM * su + 512 * (dmh + 1)],
                    p3o[s4][:], wsel[:, col:col + 1],
                    out_acc[:, DM * su + 512 * dmh:DM * su + 512 * (dmh + 1)],
                    ALU.mult, ALU.add).then_inc(SM["o3f"].h, 1))
                inc(SM["o3f"])
                if e == E - 1 and c == NCF - 1 and dmh == 1:
                    su_done[16 * w + su] = SM["o3f"].v
            o3f_hist.append(SM["o3f"].v)
        pair_o3f.append(SM["o3f"].v)
        if q == 3:
            chunk_o3f.append(SM["o3f"].v)

    NP3 = NW * E * NCF * NQ
    for p in range(NP3):
        emitA(p)
        if p >= 1:
            emitB(p - 1)
    emitB(NP3 - 1)

    for w in range(NW):
        for su in range(16):
            W(SY, SM["o3f"], su_done[16 * w + su])
            dma(SY, out_d.rearrange("(s p) d -> s p d", p=128)[16 * w + su],
                out_acc[:, DM * su:DM * (su + 1)], SM["st"])
            if w == 0:
                W(SY, SM["st"], SM["st"].v)
                dma(SY, out_acc[:, DM * su:DM * (su + 1)],
                    base_d.rearrange("(s p) d -> s p d", p=128)[16 + su], SM["bi"])

    W(T, SM["st"], 256 * NW)

    block = es.enter_context(nc.Block())

    @block.sync
    def _(sync):
        P.run(SY, sync)

    @block.gpsimd
    def _(gp):
        P.run(GP, gp)

    @block.tensor
    def _(t):
        P.run(T, t)

    @block.vector
    def _(v):
        P.run(V, v)

    @block.scalar
    def _(s):
        P.run(S, s)

    es.close()
    return nc


_NC_CACHE = None


def kernel(L0_attn_x, L0_attn_w_qkv, L0_attn_b_qkv, L0_attn_w_proj, L0_attn_b_proj,
           L0_ffn_router_w, L0_ffn_router_b, L0_ffn_temperature,
           L0_ffn_wg, L0_ffn_wp, L0_ffn_wo):
    global _NC_CACHE
    x = np.ascontiguousarray(np.asarray(L0_attn_x, dtype=np.float32))
    assert x.shape == (8, NTOK, DM)
    common = {
        "w_qkv": np.ascontiguousarray(np.asarray(L0_attn_w_qkv, np.float32)),
        "b_qkv": np.ascontiguousarray(np.asarray(L0_attn_b_qkv, np.float32)),
        "w_proj": np.ascontiguousarray(np.asarray(L0_attn_w_proj, np.float32)),
        "b_proj": np.ascontiguousarray(np.asarray(L0_attn_b_proj, np.float32)),
        "router_w": np.ascontiguousarray(np.asarray(L0_ffn_router_w, np.float32)),
        "router_b": np.ascontiguousarray(np.asarray(L0_ffn_router_b, np.float32)),
        "temperature": np.ascontiguousarray(np.asarray(L0_ffn_temperature, np.float32)),
        "wg": np.ascontiguousarray(np.asarray(L0_ffn_wg, np.float32)),
        "wp": np.ascontiguousarray(np.asarray(L0_ffn_wp, np.float32)),
        "wo": np.ascontiguousarray(np.asarray(L0_ffn_wo, np.float32)),
        "identb": np.eye(128, dtype=ml_dtypes.bfloat16),
    }
    if _NC_CACHE is None:
        _NC_CACHE = build_program()
    nc = _NC_CACHE
    in_maps = [dict(common, x=np.ascontiguousarray(x[i])) for i in range(8)]
    res = run_bass_kernel_spmd(nc, in_maps, core_ids=list(range(8)))
    return np.stack([r["out"] for r in res.results], axis=0)


# version 19
# speedup vs baseline: 1.1353x; 1.1353x over previous
import numpy as np
import ml_dtypes
from contextlib import ExitStack
import concourse.bass as bass
import concourse.mybir as mybir
from concourse.bass_utils import run_bass_kernel_spmd

F32 = mybir.dt.float32
BF16 = mybir.dt.bfloat16
AF = mybir.ActivationFunctionType
ALU = mybir.AluOpType
AX = mybir.AxisListType

NTOK = 4096
DM = 1024
E = 8
FF = 2048
NSUB = NTOK // 128
WAVE = 2048
NW = NTOK // WAVE
NQ = WAVE // 512
NCF = FF // 512
EPS_LN = 1e-5
KVW = 66

TOP = 229344
A0 = TOP - 65536
B0 = A0 - 65536
C0 = B0 - 61440
CW = C0 + 36864

T, V, S, SY, GP = "tensor", "vector", "scalar", "sync", "gpsimd"
ENGS = [T, V, S, SY, GP]


class Sem:
    def __init__(self, nc, es, name):
        self.h = es.enter_context(nc.semaphore(name))
        self.v = 0
        self.name = name


class Prog:
    def __init__(self):
        self.ops = {e: [] for e in ENGS}

    def emit(self, eng, fn):
        self.ops[eng].append(fn)

    def wait(self, eng, sem, val=None):
        v = sem.v if val is None else val
        if v > 0:
            self.ops[eng].append(lambda e, h=sem.h, v=v: e.wait_ge(h, v))

    def run(self, eng, engine_obj):
        for fn in self.ops[eng]:
            fn(engine_obj)


def build_program():
    nc = bass.Bass("TRN2", target_bir_lowering=False, debug=False)
    assert nc.sbuf_top == TOP, nc.sbuf_top

    x_d = nc.dram_tensor("x", [NTOK, DM], F32, kind="ExternalInput").ap()
    wqkv_d = nc.dram_tensor("w_qkv", [DM, 3 * DM], F32, kind="ExternalInput").ap()
    bqkv_d = nc.dram_tensor("b_qkv", [3 * DM], F32, kind="ExternalInput").ap()
    wproj_d = nc.dram_tensor("w_proj", [DM, DM], F32, kind="ExternalInput").ap()
    bproj_d = nc.dram_tensor("b_proj", [DM], F32, kind="ExternalInput").ap()
    rw_d = nc.dram_tensor("router_w", [E, DM], F32, kind="ExternalInput").ap()
    rb_d = nc.dram_tensor("router_b", [E], F32, kind="ExternalInput").ap()
    temp_d = nc.dram_tensor("temperature", [1], F32, kind="ExternalInput").ap()
    wg_d = nc.dram_tensor("wg", [E, DM, FF], F32, kind="ExternalInput").ap()
    wp_d = nc.dram_tensor("wp", [E, DM, FF], F32, kind="ExternalInput").ap()
    wo_d = nc.dram_tensor("wo", [E, FF, DM], F32, kind="ExternalInput").ap()
    identb_d = nc.dram_tensor("identb", [128, 128], BF16, kind="ExternalInput").ap()
    out_d = nc.dram_tensor("out", [NTOK, DM], F32, kind="ExternalOutput").ap()
    base_d = nc.dram_tensor("base_scr", [NTOK, DM], F32, kind="Internal").ap()

    es = ExitStack()
    sb = lambda name, shape, dt=F32: es.enter_context(nc.sbuf_tensor(name, shape, dt))
    man = lambda name, shape, dt, off: nc.alloc_sbuf_tensor_at(name, shape, dt, offset=off)

    identb = sb("identb_s", [128, 128], BF16)
    rwT = sb("rwT", [128, 8 * E], BF16)
    tmp1 = sb("tmp1", [1, 4])
    rtA = sb("rtA", [128, 2])
    rtemp = sb("rtemp", [128, 1])
    bqkv_row = sb("bqkv_row", [1, 3 * DM], BF16)
    bproj_row = sb("bproj_row", [1, DM], BF16)
    rb_row = sb("rb_row", [1, E], BF16)
    onesrow = sb("onesrow", [1, 512], BF16)
    onesf1 = sb("onesf1", [1, 128])
    onesb = sb("onesb", [128, 1], BF16)
    onesf8 = sb("onesf8", [8, 128])
    onescol = sb("onescol", [128, 1])
    eye8 = sb("eye8", [8, 8])
    kvbd = sb("kvbd", [128, 8 * 130], BF16)
    acH = sb("acH", [128, DM], BF16)
    scores = sb("scores", [128, 256])
    wsel = sb("wsel", [128, 256])
    skE = sb("skE", [128, 256])
    sk2 = sb("sk2", [128, 256])
    sk32 = sb("sk32", [128, 32])
    sk32b = sb("sk32b", [128, 32])
    sk32c = sb("sk32c", [128, 32])
    sk32d = sb("sk32d", [128, 32])
    sk8 = sb("sk8", [128, 8])
    skL8 = sb("skL8", [8, 1])
    skL8b = sb("skL8b", [8, 1])
    skdg = sb("skdg", [8, 8])
    z16b = sb("z16b", [128, 16])
    z16c = sb("z16c", [128, 16])
    lnc = sb("lnc", [128, 16])
    mv12 = sb("mv12", [128, 24])
    mv2 = sb("mv2", [128, 4])

    q_sb = man("q_sb", [128, 8 * NTOK], BF16, A0)
    out_acc = man("out_acc", [128, 16 * DM], F32, A0)
    wqkv_sb = man("wqkv_sb", [128, 8 * 3072], BF16, B0)
    xt_sb = [man(f"xt{i}", [128, DM], F32, B0 + 49152 + 4096 * i) for i in range(2)]
    h2T = man("h2T", [128, 8 * NTOK], BF16, B0)
    h1b = [man("h1b0", [128, DM], BF16, C0), man("h1b1", [128, DM], BF16, C0 + 26624)]
    h1T = man("h1T", [128, 8 * 512], BF16, C0 + 2048)
    kb_bf = [man(f"kb{i}", [128, DM], BF16, C0 + 10240 + 2048 * i) for i in range(2)]
    vb_bf = [man(f"vb{i}", [128, DM], BF16, C0 + 14336 + 2048 * i) for i in range(2)]
    qtmp1 = man("qtmp1", [128, 512], F32, C0 + 18432)
    qtmp2 = man("qtmp2", [128, 512], F32, C0 + 20480)
    qtmp3 = man("qtmp3", [128, 512], F32, C0 + 22528)
    rws = man("rws", [E, DM], BF16, C0 + 24576)
    assert C0 + 26624 <= CW
    wproj_sb = man("wproj_sb", [128, 8 * DM], BF16, C0)
    xt2_sb = [man(f"xt2_{i}", [128, DM], F32, C0 + 16384 + 4096 * i) for i in range(2)]
    x1_sb = [man(f"x1_{i}", [128, DM], F32, C0 + 24576 + 4096 * i) for i in range(2)]
    ac_bf = man("ac_bf", [128, DM], BF16, C0 + 32768)
    aT_bf = man("aT_bf", [128, DM], BF16, C0 + 34816)
    wg_sb = [man("wg_p0", [128, 8 * 512], BF16, CW),
             man("wg_p1", [128, 8 * 512], BF16, C0)]
    wp_sb = [man("wp_p0", [128, 8 * 512], BF16, CW + 8192),
             man("wp_p1", [128, 8 * 512], BF16, C0 + 8192)]
    wo_sb = [man("wo_p0", [128, 4 * DM], BF16, CW + 16384),
             man("wo_p1", [128, 4 * DM], BF16, C0 + 16384)]
    t_sb = [man(f"t3_{i}", [128, 4 * 512], BF16, C0 + 24576 + 4096 * i) for i in range(2)]
    rt_sb = [man(f"rt3_{i}", [128, 512], F32, C0 + 32768 + 2048 * i) for i in range(2)]
    assert nc.sbuf_base <= C0, f"bottom region overflow: {nc.sbuf_base} > {C0}"

    esp = ExitStack()
    psf = lambda name, shape, dt=F32: esp.enter_context(nc.psum_tensor(name, shape, dt))
    p1q = [psf(f"p1q{i}", [128, 512]) for i in range(2)]
    p1kv = [psf(f"p1kv{i}", [128, 512]) for i in range(2)]
    p1kvx = [psf(f"p1kvx{i}", [128, 4 * KVW]) for i in range(2)]
    p1tr = [psf(f"p1tr{i}", [128, 128], BF16) for i in range(2)]
    esp.close()
    esp = ExitStack()
    psf = lambda name, shape, dt=F32: esp.enter_context(nc.psum_tensor(name, shape, dt))
    p2at = [psf(f"p2at{i}", [128, 260]) for i in range(2)]
    p2pr = [psf(f"p2pr{i}", [128, 512]) for i in range(2)]
    p2tr = [psf(f"p2tr{i}", [128, 128], BF16) for i in range(2)]
    p2sc = psf("p2sc", [128, 8])
    p2bc = psf("p2bc", [128, 1])
    esp.close()
    esp = ExitStack()
    psf = lambda name, shape, dt=F32: esp.enter_context(nc.psum_tensor(name, shape, dt))
    psk1 = psf("psk1", [8, 1])
    psk2 = psf("psk2", [128, 8])
    esp.close()
    esp = ExitStack()
    psf = lambda name, shape, dt=F32: esp.enter_context(nc.psum_tensor(name, shape, dt))
    p3g = [psf(f"p3g{i}", [128, 512]) for i in range(2)]
    p3p = [psf(f"p3p{i}", [128, 512]) for i in range(2)]
    p3o = [psf(f"p3o{i}", [128, 512]) for i in range(4)]
    esp.close()

    SM = {}
    for nm in ["init", "wqkv", "wproj", "rw", "brow", "x0", "x1p", "x2a", "x2b",
               "xf", "lnA", "lnB", "h1", "tp", "tpf", "q", "qf", "kv", "kvf",
               "kbf", "kvb", "at", "atf", "ac", "pr", "prf", "b2",
               "bst", "bst1", "h2T", "sc", "scf", "skv", "sks", "skt", "sktf",
               "w3a", "w3b", "w3f", "gp", "gpf", "rl", "t3", "t3f", "o3",
               "o3f", "bi", "st", "cst", "vv", "ss", "tpd", "tpd1", "trad", "trhd",
               "wq2"]:
        SM[nm] = Sem(nc, es, "s_" + nm)
    sx = [SM["x0"], SM["x1p"]]
    sx2 = [SM["x2a"], SM["x2b"]]
    sbst = [SM["bst"], SM["bst1"]]
    sw3 = [SM["w3a"], SM["w3b"]]

    stpd = None
    P = Prog()
    W = P.wait
    EM = P.emit

    def inc(sem, n=1):
        sem.v += n
        return sem.v

    def dma(eng, out, in_, sem, slow=False, tr=False):
        EM(eng, lambda e, o=out, i=in_, h=sem.h, sl=slow, tr=tr: e.dma_start(
            out=o, in_=i, allow_slow_non_contiguous=sl, transpose=tr).then_inc(h, 16))
        inc(sem, 16)

    def vsem(fn):
        EM(V, lambda v: fn(v).then_inc(SM["vv"].h, 1))
        inc(SM["vv"])
        W(V, SM["vv"])

    def ssem(fn):
        EM(S, lambda s: fn(s).then_inc(SM["ss"].h, 1))
        inc(SM["ss"])
        W(S, SM["ss"])

    dma(SY, identb[:], identb_d, SM["init"])
    dma(SY, tmp1[0:1, 0:1], temp_d.unsqueeze(0), SM["init"])
    dma(GP, bqkv_row[:], bqkv_d.unsqueeze(0), SM["brow"])
    dma(GP, bproj_row[:], bproj_d.unsqueeze(0), SM["brow"])
    dma(GP, rb_row[:], rb_d.unsqueeze(0), SM["brow"])
    dma(GP, rws[:], rw_d, SM["rw"])
    dma(GP, wqkv_sb[:].rearrange("p (k c) -> p k c", c=3072)[:, :, 1024:3072],
        wqkv_d.rearrange("(k p) c -> p k c", p=128)[:, :, 1024:3072], SM["wqkv"])
    dma(GP, wqkv_sb[:].rearrange("p (k c) -> p k c", c=3072)[:, :, 0:1024],
        wqkv_d.rearrange("(k p) c -> p k c", p=128)[:, :, 0:1024], SM["wq2"])

    def wchunk(e, c, b):
        dma(GP, wg_sb[b][:].rearrange("p (k f) -> p k f", f=512),
            wg_d[e].rearrange("(k p) f -> p k f", p=128)[:, :, 512 * c:512 * (c + 1)], sw3[b])
        dma(GP, wp_sb[b][:].rearrange("p (k f) -> p k f", f=512),
            wp_d[e].rearrange("(k p) f -> p k f", p=128)[:, :, 512 * c:512 * (c + 1)], sw3[b])
        dma(GP, wo_sb[b][:].rearrange("p (f d) -> p f d", d=DM),
            wo_d[e].rearrange("(f p) d -> p f d", p=128)[:, 4 * c:4 * (c + 1), :], sw3[b])
    wchunk(0, 0, 0)

    EM(V, lambda v: v.memset(onesrow[:], 1.0))
    EM(V, lambda v: v.memset(onesf1[:], 1.0))
    EM(V, lambda v: v.memset(onesb[:], 1.0))
    EM(V, lambda v: v.memset(onesf8[:], 1.0))
    EM(V, lambda v: v.memset(onescol[:], 1.0))
    EM(V, lambda v: v.memset(p1kvx[0][:], 0.0))
    EM(V, lambda v: v.memset(p1kvx[1][:], 0.0))
    W(V, SM["init"])
    EM(V, lambda v: v.tensor_copy(eye8[:], identb[0:8, 0:8]).then_inc(SM["cst"].h, 1))
    inc(SM["cst"])
    W(T, SM["init"])
    W(T, SM["cst"])
    EM(T, lambda t: t.matmul(p2bc[:], onesf1[:], tmp1[0:1, 0:1], start=True, stop=True).then_inc(SM["skt"].h, 1))
    inc(SM["skt"])
    W(V, SM["skt"])
    vsem(lambda v: v.tensor_copy(rtA[:, 0:1], p2bc[:]))
    vsem(lambda v: v.tensor_scalar(rtA[:, 1:2], rtA[:, 0:1], 0.1, None, ALU.max))
    EM(V, lambda v: v.reciprocal(rtemp[:], rtA[:, 1:2]).then_inc(SM["sktf"].h, 1))
    inc(SM["sktf"])
    W(T, SM["rw"])
    W(T, SM["sktf"])
    for k in range(8):
        if k >= 2:
            W(T, SM["sktf"], SM["sktf"].v - 1)
        EM(T, lambda t, k=k: t.transpose(p1tr[k % 2][:, 0:8], rws[:, 128 * k:128 * (k + 1)],
                                         identb[0:8, 0:8]).then_inc(SM["skt"].h, 1))
        inc(SM["skt"])
        W(V, SM["skt"])
        EM(V, lambda v, k=k: v.tensor_copy(rwT[:, 8 * k:8 * (k + 1)], p1tr[k % 2][:, 0:8]).then_inc(SM["sktf"].h, 1))
        inc(SM["sktf"])

    stpd = [SM["tpd"], SM["tpd1"]]
    tpd_hist = []
    qf_hist = []
    kvf_hist = []
    lnpar = 8

    W(T, SM["wqkv"])
    W(T, SM["brow"], 48)
    W(T, SM["sktf"])

    def p1_ln_v(i):
        pr = i % 2
        if i >= 2:
            W(SY, SM["h1"], i - 1)
        dma(SY, xt_sb[pr][:], x_d.rearrange("(s p) d -> s p d", p=128)[i], sx[pr])
        W(V, sx[pr])
        if i >= 2:
            W(V, SM["h1"], i - 1)
        c0 = lnpar * pr
        m0 = 12 * pr
        EM(V, lambda v, pr=pr, m0=m0: v.bn_stats(mv12[:, m0:m0 + 6], xt_sb[pr][:, 0:512]))
        vsem(lambda v, pr=pr, m0=m0: v.bn_stats(mv12[:, m0 + 6:m0 + 12], xt_sb[pr][:, 512:1024]))
        vsem(lambda v, pr=pr, m0=m0: v.bn_aggr(mv2[:, 2 * pr:2 * pr + 2], mv12[:, m0:m0 + 12]))
        EM(V, lambda v, pr=pr, c0=c0: v.tensor_scalar(lnc[:, c0:c0 + 1], mv2[:, 2 * pr + 1:2 * pr + 2],
                                                      EPS_LN, None, ALU.add).then_inc(SM["lnA"].h, 1))
        inc(SM["lnA"])
        W(S, SM["lnA"])
        ssem(lambda s, c0=c0: s.activation(lnc[:, c0 + 1:c0 + 2], lnc[:, c0:c0 + 1], AF.Ln))
        EM(S, lambda s, c0=c0: s.activation(lnc[:, c0 + 2:c0 + 3], lnc[:, c0 + 1:c0 + 2],
                                            AF.Exp, scale=-0.5).then_inc(SM["lnB"].h, 1))
        inc(SM["lnB"])
        W(S, SM["lnB"])
        W(V, SM["lnB"])
        EM(V, lambda v, c0=c0, pr=pr: v.scalar_tensor_tensor(lnc[:, c0 + 3:c0 + 4], mv2[:, 2 * pr:2 * pr + 1],
                                                             -1.0, lnc[:, c0 + 2:c0 + 3],
                                                             ALU.mult, ALU.mult).then_inc(SM["lnA"].h, 1))
        inc(SM["lnA"])

    def p1_ln_s(i):
        pr = i % 2
        c0 = lnpar * pr
        W(S, SM["lnA"])
        if i >= 2:
            W(S, stpd[tpd_hist[i - 2][0]], tpd_hist[i - 2][1])
        EM(S, lambda s, pr=pr, c0=c0, i=i: s.activation(h1b[i % 2][:], xt_sb[pr][:], AF.Identity,
                                                   bias=lnc[:, c0 + 3:c0 + 4],
                                                   scale=lnc[:, c0 + 2:c0 + 3]).then_inc(SM["h1"].h, 1))
        inc(SM["h1"])

    def p1_tr(i):
        sup, s4 = divmod(i, 4)
        W(SY, SM["h1"], i + 1)
        if sup >= 1:
            W(SY, SM["q"], 8 * sup)
        for j in range(8):
            dma(SY, h1T[:, 512 * j + 128 * s4:512 * j + 128 * (s4 + 1)],
                h1b[i % 2][:, 128 * j:128 * (j + 1)], stpd[i % 2], tr=True)
        tpd_hist.append((i % 2, stpd[i % 2].v))

    def p1_kvkvx(i):
        sup, s4 = divmod(i, 4)
        pr = i % 2
        W(T, stpd[tpd_hist[i][0]], tpd_hist[i][1])
        for cc in range(4):
            kvi = i * 4 + cc
            if kvi >= 2:
                W(T, SM["kvf"], kvf_hist[kvi - 2])
            pb = p1kv[kvi % 2]
            for k in range(8):
                EM(T, lambda t, pb=pb, k=k, s4=s4, cc=cc: t.matmul(
                    pb[:], h1T[:, 512 * k + 128 * s4:512 * k + 128 * (s4 + 1)],
                    wqkv_sb[:, 3072 * k + 1024 + 512 * cc:3072 * k + 1024 + 512 * (cc + 1)],
                    start=(k == 0), stop=False))
            EM(T, lambda t, pb=pb, cc=cc: t.matmul(
                pb[:], onesrow[0:1, 0:128],
                bqkv_row[0:1, 1024 + 512 * cc:1024 + 512 * (cc + 1)],
                start=False, stop=True).then_inc(SM["kv"].h, 1))
            inc(SM["kv"])
            if cc < 2:
                W(V, SM["kv"])
                W(V, SM["sks"])
                EM(V, lambda v, pb=pb, cc=cc: v.tensor_single_scalar(
                    qtmp1[:] if cc == 0 else qtmp2[:], pb[:], 0.0, ALU.min).then_inc(SM["skv"].h, 1))
                inc(SM["skv"])
                W(S, SM["skv"])
                W(S, SM["kvf"])
                W(S, SM["qf"])
                EM(S, lambda s, cc=cc: s.activation(qtmp3[:],
                                                    qtmp1[:] if cc == 0 else qtmp2[:],
                                                    AF.Exp).then_inc(SM["sks"].h, 1))
                inc(SM["sks"])
                W(V, SM["sks"])
                EM(V, lambda v, pb=pb, i=i, cc=cc: v.scalar_tensor_tensor(
                    kb_bf[i % 2][:, 512 * cc:512 * (cc + 1)], pb[:], 0.0, qtmp3[:],
                    ALU.max, ALU.add).then_inc(SM["kvf"].h, 1))
            else:
                W(V, SM["kv"])
                EM(V, lambda v, pb=pb, i=i, cc=cc: v.tensor_copy(
                    vb_bf[i % 2][:, 512 * (cc - 2):512 * (cc - 1)], pb[:]).then_inc(SM["kvf"].h, 1))
            kvf_hist.append(inc(SM["kvf"]))
        W(T, SM["kvf"])
        for pair in range(8):
            g, jj = divmod(pair, 4)
            for half in range(2):
                hd0 = 128 * pair + 64 * half
                last = pair == 7 and half == 1
                EM(T, lambda t, g=g, jj=jj, half=half, hd0=hd0, i=i: t.matmul(
                    p1kvx[g][64 * half:64 * (half + 1), KVW * jj:KVW * jj + 64],
                    kb_bf[i % 2][:, hd0:hd0 + 64], vb_bf[i % 2][:, hd0:hd0 + 64],
                    start=False, stop=False, skip_group_check=True))
                mmfn = lambda t, g=g, jj=jj, half=half, hd0=hd0, i=i: t.matmul(
                    p1kvx[g][64 * half:64 * (half + 1), KVW * jj + 64:KVW * jj + 65],
                    kb_bf[i % 2][:, hd0:hd0 + 64], onesb[:],
                    start=False, stop=False, skip_group_check=True)
                if last:
                    EM(T, lambda t, f=mmfn: f(t).then_inc(SM["kbf"].h, 1))
                else:
                    EM(T, mmfn)
        inc(SM["kbf"])

    def p1_q(sup):
        if sup == 0:
            W(T, SM["wq2"])
        for qc in range(8):
            qi = sup * 8 + qc
            if qi >= 2:
                W(T, SM["qf"], qf_hist[qi - 2])
            for k in range(8):
                EM(T, lambda t, qc=qc, k=k, qi=qi: t.matmul(
                    p1q[qi % 2][:], wqkv_sb[:, 3072 * k + 128 * qc:3072 * k + 128 * (qc + 1)],
                    h1T[:, 512 * k:512 * (k + 1)], start=(k == 0), stop=False))
            EM(T, lambda t, qc=qc, qi=qi: t.matmul(
                p1q[qi % 2][:], bqkv_row[0:1, 128 * qc:128 * (qc + 1)],
                onesrow[0:1, 0:512], start=False, stop=True).then_inc(SM["q"].h, 1))
            inc(SM["q"])
            W(V, SM["q"])
            W(V, SM["sks"])
            EM(V, lambda v, qi=qi, qc=qc: v.tensor_single_scalar(
                qtmp1[:] if qc % 2 == 0 else qtmp2[:], p1q[qi % 2][:], 0.0,
                ALU.min).then_inc(SM["skv"].h, 1))
            inc(SM["skv"])
            W(S, SM["skv"])
            W(S, SM["kvf"])
            W(S, SM["qf"])
            EM(S, lambda s, qc=qc: s.activation(qtmp3[:],
                                                qtmp1[:] if qc % 2 == 0 else qtmp2[:],
                                                AF.Exp).then_inc(SM["sks"].h, 1))
            inc(SM["sks"])
            W(V, SM["sks"])
            EM(V, lambda v, qc=qc, sup=sup, qi=qi: v.scalar_tensor_tensor(
                q_sb[:, NTOK * qc + 512 * sup:NTOK * qc + 512 * (sup + 1)],
                p1q[qi % 2][:], 0.0, qtmp3[:], ALU.max, ALU.add).then_inc(SM["qf"].h, 1))
            qf_hist.append(inc(SM["qf"]))

    p1_ln_v(0)
    p1_ln_s(0)
    p1_ln_v(1)
    p1_ln_s(1)
    for i in range(NSUB):
        if i + 2 < NSUB:
            p1_ln_v(i + 2)
        p1_tr(i)
        if i + 2 < NSUB:
            p1_ln_s(i + 2)
        p1_kvkvx(i)
        if i % 4 == 3:
            p1_q(i // 4)

    W(V, SM["kbf"])
    EM(V, lambda v: v.memset(kvbd[:], 0.0).then_inc(SM["skv"].h, 1))
    inc(SM["skv"])
    W(V, SM["skv"])
    for pair in range(8):
        g, jj = divmod(pair, 4)
        EM(V, lambda v, pair=pair, g=g, jj=jj: v.tensor_copy(
            kvbd[0:64, 130 * pair:130 * pair + 64], p1kvx[g][0:64, KVW * jj:KVW * jj + 64]))
        EM(V, lambda v, pair=pair, g=g, jj=jj: v.tensor_copy(
            kvbd[0:64, 130 * pair + 64:130 * pair + 65], p1kvx[g][0:64, KVW * jj + 64:KVW * jj + 65]))
        EM(V, lambda v, pair=pair, g=g, jj=jj: v.tensor_copy(
            kvbd[64:128, 130 * pair + 65:130 * pair + 129], p1kvx[g][64:128, KVW * jj:KVW * jj + 64]))
        if pair < 7:
            EM(V, lambda v, pair=pair, g=g, jj=jj: v.tensor_copy(
                kvbd[64:128, 130 * pair + 129:130 * pair + 130], p1kvx[g][64:128, KVW * jj + 64:KVW * jj + 65]))
        else:
            EM(V, lambda v, pair=pair, g=g, jj=jj: v.tensor_copy(
                kvbd[64:128, 130 * pair + 129:130 * pair + 130],
                p1kvx[g][64:128, KVW * jj + 64:KVW * jj + 65]).then_inc(SM["kvb"].h, 1))
    inc(SM["kvb"])

    W(GP, SM["qf"], qf_hist[-1])
    W(GP, SM["kbf"])
    dma(GP, wproj_sb[:].rearrange("p (k c) -> p k c", c=DM),
        wproj_d.rearrange("(k p) c -> p k c", p=128), SM["wproj"])

    atf_hist = []
    prf_hist = []
    scf_hist = []
    b2_hist = []
    bst_hist = []
    trad_hist = []
    trhd_hist = []
    p2tpf_hist = []
    p2seq = [0]

    W(T, SM["kvb"])
    W(T, SM["wproj"])

    def p2_transpose_grp(srcbuf, dst_slices):
        for j in range(8):
            sq = p2seq[0]
            p2seq[0] += 1
            if sq >= 2:
                W(T, SM["tpf"], p2tpf_hist[sq - 2])
            EM(T, lambda t, j=j, sq=sq, srcbuf=srcbuf: t.transpose(
                p2tr[sq % 2][:], srcbuf[:, 128 * j:128 * (j + 1)], identb[:]).then_inc(SM["tp"].h, 1))
            inc(SM["tp"])
            W(V, SM["tp"])
            EM(V, lambda v, j=j, sq=sq, d=dst_slices[j]: v.tensor_copy(
                d, p2tr[sq % 2][:]).then_inc(SM["tpf"].h, 1))
            p2tpf_hist.append(inc(SM["tpf"]))

    def p2_at(i):
        pr = i % 2
        if i < 2:
            W(SY, SM["qf"], qf_hist[-1])
            W(SY, SM["kbf"])
        else:
            W(SY, sbst[pr], bst_hist[i - 2][pr])
        dma(SY, xt2_sb[pr][:], x_d.rearrange("(s p) d -> s p d", p=128)[i], sx2[pr])
        for g4 in range(4):
            agi = i * 4 + g4
            if agi >= 2:
                W(T, SM["atf"], atf_hist[agi - 2])
            for u in range(2):
                pair = g4 * 2 + u
                mmfn = lambda t, g4=g4, u=u, pair=pair, i=i: t.matmul(
                    p2at[g4 % 2][:, 130 * u:130 * (u + 1)],
                    q_sb[:, NTOK * pair + 128 * i:NTOK * pair + 128 * (i + 1)],
                    kvbd[:, 130 * pair:130 * (pair + 1)], start=True, stop=True)
                if u == 1:
                    EM(T, lambda t, f=mmfn: f(t).then_inc(SM["at"].h, 1))
                else:
                    EM(T, mmfn)
            inc(SM["at"])
            W(V, SM["at"])
            if g4 == 0 and i >= 1:
                W(V, SM["tpf"], trad_hist[i - 1])
            pa = p2at[g4 % 2]
            pav = pa[:].rearrange("p (u h c) -> p u h c", h=2, c=65)
            vsem(lambda v, pav=pav, g4=g4: v.tensor_scalar(
                z16b[:, 4 * g4:4 * g4 + 4].rearrange("p (u h) -> p u h", h=2).unsqueeze(3),
                pav[:, :, :, 64:65], 1e-8, None, ALU.add))
            vsem(lambda v, g4=g4: v.reciprocal(z16c[:, 4 * g4:4 * g4 + 4], z16b[:, 4 * g4:4 * g4 + 4]))
            EM(V, lambda v, pav=pav, g4=g4: v.tensor_tensor(
                ac_bf[:, 256 * g4:256 * (g4 + 1)].rearrange("p (u h d) -> p u h d", h=2, d=64),
                pav[:, :, :, 0:64],
                z16c[:, 4 * g4:4 * g4 + 4].rearrange("p (u h) -> p u h", h=2).unsqueeze(3).broadcast_to([128, 2, 2, 64]),
                ALU.mult).then_inc(SM["atf"].h, 1))
            atf_hist.append(inc(SM["atf"]))

    def p2_tra(i):
        W(T, SM["atf"], atf_hist[i * 4 + 3])
        p2_transpose_grp(ac_bf, [aT_bf[:, 128 * j:128 * (j + 1)] for j in range(8)])
        trad_hist.append(SM["tpf"].v)

    def p2_pj(i):
        pr = i % 2
        c0 = lnpar * pr
        m0 = 12 * pr
        W(T, SM["tpf"], trad_hist[i])
        for dmh in range(2):
            pri = i * 2 + dmh
            if pri >= 2:
                W(T, SM["prf"], prf_hist[pri - 2])
            for k in range(8):
                EM(T, lambda t, k=k, dmh=dmh, pri=pri: t.matmul(
                    p2pr[pri % 2][:], aT_bf[:, 128 * k:128 * (k + 1)],
                    wproj_sb[:, DM * k + 512 * dmh:DM * k + 512 * (dmh + 1)],
                    start=(k == 0), stop=False))
            EM(T, lambda t, dmh=dmh, pri=pri: t.matmul(
                p2pr[pri % 2][:], onesrow[0:1, 0:128],
                bproj_row[0:1, 512 * dmh:512 * (dmh + 1)], start=False, stop=True).then_inc(SM["pr"].h, 1))
            inc(SM["pr"])
            W(V, SM["pr"])
            W(V, sx2[pr])
            if i >= 2:
                W(V, SM["b2"], b2_hist[i - 2])
            EM(V, lambda v, pr=pr, dmh=dmh, pri=pri: v.scalar_tensor_tensor(
                x1_sb[pr][:, 512 * dmh:512 * (dmh + 1)], p2pr[pri % 2][:], 1.0,
                xt2_sb[pr][:, 512 * dmh:512 * (dmh + 1)], ALU.mult, ALU.add).then_inc(SM["prf"].h, 1))
            prf_hist.append(inc(SM["prf"]))
        W(V, SM["prf"])
        EM(V, lambda v, pr=pr, m0=m0: v.bn_stats(mv12[:, m0:m0 + 6], x1_sb[pr][:, 0:512]))
        vsem(lambda v, pr=pr, m0=m0: v.bn_stats(mv12[:, m0 + 6:m0 + 12], x1_sb[pr][:, 512:1024]))
        vsem(lambda v, pr=pr, m0=m0: v.bn_aggr(mv2[:, 2 * pr:2 * pr + 2], mv12[:, m0:m0 + 12]))
        EM(V, lambda v, pr=pr, c0=c0: v.tensor_scalar(lnc[:, c0:c0 + 1], mv2[:, 2 * pr + 1:2 * pr + 2],
                                                      EPS_LN, None, ALU.add).then_inc(SM["lnA"].h, 1))
        inc(SM["lnA"])
        W(S, SM["lnA"])
        ssem(lambda s, c0=c0: s.activation(lnc[:, c0 + 1:c0 + 2], lnc[:, c0:c0 + 1], AF.Ln))
        EM(S, lambda s, c0=c0: s.activation(lnc[:, c0 + 2:c0 + 3], lnc[:, c0 + 1:c0 + 2],
                                            AF.Exp, scale=-0.5).then_inc(SM["lnB"].h, 1))
        inc(SM["lnB"])
        W(S, SM["lnB"])
        W(V, SM["lnB"])
        EM(V, lambda v, c0=c0, pr=pr: v.scalar_tensor_tensor(lnc[:, c0 + 3:c0 + 4], mv2[:, 2 * pr:2 * pr + 1],
                                                             -1.0, lnc[:, c0 + 2:c0 + 3],
                                                             ALU.mult, ALU.mult))
        EM(V, lambda v, c0=c0: v.tensor_scalar(lnc[:, c0 + 4:c0 + 5], lnc[:, c0 + 2:c0 + 3],
                                               1.0, None, ALU.add).then_inc(SM["lnA"].h, 1))
        inc(SM["lnA"])
        W(S, SM["lnA"])
        if i >= 1:
            W(S, SM["tpf"], trhd_hist[i - 1])
        EM(S, lambda s, pr=pr, c0=c0: s.activation(acH[:], x1_sb[pr][:], AF.Identity,
                                                   bias=lnc[:, c0 + 3:c0 + 4],
                                                   scale=lnc[:, c0 + 2:c0 + 3]))
        EM(S, lambda s, pr=pr, c0=c0: s.activation(xt2_sb[pr][:], x1_sb[pr][:], AF.Identity,
                                                   bias=lnc[:, c0 + 3:c0 + 4],
                                                   scale=lnc[:, c0 + 4:c0 + 5]).then_inc(SM["b2"].h, 1))
        b2_hist.append(inc(SM["b2"]))
        W(SY, SM["b2"])
        dma(SY, base_d.rearrange("(s p) d -> s p d", p=128)[i], xt2_sb[pr][:], sbst[pr])
        bst_hist.append((SM["bst"].v, SM["bst1"].v))

    def p2_trh(i):
        W(T, SM["b2"], b2_hist[i])
        p2_transpose_grp(acH, [h2T[:, NTOK * j + 128 * i:NTOK * j + 128 * (i + 1)] for j in range(8)])
        trhd_hist.append(SM["tpf"].v)

    def p2_rt(i):
        W(T, SM["tpf"], trhd_hist[i])
        if i >= 1:
            W(T, SM["scf"], scf_hist[i - 1])
        for k in range(8):
            EM(T, lambda t, k=k, i=i: t.matmul(p2sc[:], h2T[:, NTOK * k + 128 * i:NTOK * k + 128 * (i + 1)],
                                               rwT[:, 8 * k:8 * (k + 1)], start=(k == 0), stop=False))
        EM(T, lambda t: t.matmul(p2sc[:], onesrow[0:1, 0:128], rb_row[:],
                                 start=False, stop=True).then_inc(SM["sc"].h, 1))
        inc(SM["sc"])
        W(V, SM["sc"])
        EM(V, lambda v, i=i: v.tensor_scalar(scores[:, 8 * i:8 * (i + 1)], p2sc[:],
                                             rtemp[:, 0:1], None, ALU.mult).then_inc(SM["scf"].h, 1))
        scf_hist.append(inc(SM["scf"]))

    for i in range(NSUB):
        if i >= 1:
            p2_tra(i - 1)
        if i >= 2:
            p2_trh(i - 2)
            p2_rt(i - 2)
        p2_at(i)
        if i >= 1:
            p2_pj(i - 1)
    p2_tra(NSUB - 1)
    p2_trh(NSUB - 2)
    p2_rt(NSUB - 2)
    p2_pj(NSUB - 1)
    p2_trh(NSUB - 1)
    p2_rt(NSUB - 1)

    r3 = lambda buf: buf[:].rearrange("p (s e) -> p s e", e=8)
    reE = lambda small: small[:].unsqueeze(2).broadcast_to([128, 32, 8])
    reS = lambda small8: small8[:].unsqueeze(1).broadcast_to([128, 32, 8])

    def v2s(fn):
        EM(V, lambda v: fn(v).then_inc(SM["skv"].h, 1))
        inc(SM["skv"])
        W(S, SM["skv"])

    def s2v(fn):
        EM(S, lambda s: fn(s).then_inc(SM["sks"].h, 1))
        inc(SM["sks"])
        W(V, SM["sks"])

    W(V, SM["scf"], scf_hist[-1])
    for it in range(3):
        vsem(lambda v: v.reduce_max(sk32[:], r3(scores), axis=AX.X))
        v2s(lambda v: v.tensor_tensor(r3(skE), r3(scores), reE(sk32), ALU.subtract))
        s2v(lambda s: s.activation(sk2[:], skE[:], AF.Exp))
        v2s(lambda v: v.reduce_sum(sk32b[:], r3(sk2), axis=AX.X))
        s2v(lambda s: s.activation(sk32c[:], sk32b[:], AF.Ln))
        v2s(lambda v: v.tensor_tensor(r3(sk2), r3(skE), reE(sk32c), ALU.subtract))
        s2v(lambda s: s.activation(wsel[:], sk2[:], AF.Exp))
        EM(V, lambda v: v.reduce_sum(sk8[:], wsel[:].rearrange("p (s e) -> p e s", e=8),
                                     axis=AX.X).then_inc(SM["skv"].h, 1))
        inc(SM["skv"])
        W(T, SM["skv"])
        if it >= 1:
            W(T, SM["sktf"])
        EM(T, lambda t: t.matmul(psk1[:], sk8[:], onescol[:], start=True, stop=True).then_inc(SM["skt"].h, 1))
        inc(SM["skt"])
        W(V, SM["skt"])
        v2s(lambda v: v.tensor_copy(skL8[:], psk1[:]))
        s2v(lambda s: s.activation(skL8b[:], skL8[:], AF.Ln))
        EM(V, lambda v: v.tensor_tensor(skdg[:], eye8[:], skL8b[:, 0:1].broadcast_to([8, 8]),
                                        ALU.mult).then_inc(SM["skv"].h, 1))
        inc(SM["skv"])
        W(T, SM["skv"])
        EM(T, lambda t: t.matmul(psk2[:], onesf8[:], skdg[:], start=True, stop=True).then_inc(SM["skt"].h, 1))
        inc(SM["skt"])
        W(V, SM["skt"])
        EM(V, lambda v: v.tensor_copy(sk8[:], psk2[:]).then_inc(SM["sktf"].h, 1))
        inc(SM["sktf"])
        W(V, SM["sktf"])
        EM(V, lambda v: v.tensor_tensor(r3(scores), r3(sk2), reS(sk8), ALU.subtract).then_inc(SM["skv"].h, 1))
        inc(SM["skv"])
        W(S, SM["skv"])
        W(V, SM["skv"])
    s2v(lambda s: s.activation(skE[:], scores[:], AF.Exp))
    vsem(lambda v: v.reduce_sum(sk32[:], r3(skE), axis=AX.X))
    vsem(lambda v: v.tensor_scalar(sk32b[:], sk32[:], 1e-8, None, ALU.add))
    vsem(lambda v: v.reciprocal(sk32c[:], sk32b[:]))
    vsem(lambda v: v.tensor_tensor(r3(sk2), r3(skE), reE(sk32c), ALU.mult))
    vsem(lambda v: v.reduce_max(sk32[:], r3(sk2), axis=AX.X))
    vsem(lambda v: v.tensor_tensor(r3(wsel), r3(sk2), reE(sk32), ALU.is_ge))
    vsem(lambda v: v.scalar_tensor_tensor(r3(skE), r3(wsel), -1e30, r3(sk2),
                                          ALU.mult, ALU.add))
    vsem(lambda v: v.reduce_max(sk32b[:], r3(skE), axis=AX.X))
    vsem(lambda v: v.tensor_tensor(sk32d[:], sk32[:], sk32b[:], ALU.add))
    vsem(lambda v: v.tensor_scalar(sk32[:], sk32d[:], 1e-8, None, ALU.add))
    vsem(lambda v: v.reciprocal(sk32d[:], sk32[:]))
    vsem(lambda v: v.tensor_tensor(r3(wsel), r3(sk2), reE(sk32b), ALU.is_ge))
    vsem(lambda v: v.tensor_tensor(r3(skE), r3(sk2), r3(wsel), ALU.mult))
    EM(V, lambda v: v.tensor_tensor(r3(wsel), r3(skE), reE(sk32d),
                                    ALU.mult).then_inc(SM["skv"].h, 1))
    inc(SM["skv"])
    wsel_done = SM["skv"].v

    h2T_done = trhd_hist[-1]
    su_done = {}
    gpf_hist = []
    pair_o3f = []
    chunk_o3f = []
    o3f_hist = []
    w3_hist = {0: [], 1: []}
    w3_hist[0].append(sw3[0].v)

    W(SY, SM["scf"], scf_hist[-1])
    W(SY, SM["bst"], bst_hist[-1][0])
    W(SY, SM["bst1"], bst_hist[-1][1])
    for su in range(16):
        dma(SY, out_acc[:, DM * su:DM * (su + 1)],
            base_d.rearrange("(s p) d -> s p d", p=128)[su], SM["bi"])
    SY_DEFERRED = True

    W(T, SM["tpf"], h2T_done)
    W(T, SM["sktf"])
    first_evac_gate_done = [False] * NW

    def emitA(p):
        ci, q = divmod(p, 4)
        w, rem = divmod(ci, E * NCF)
        e, c = divmod(rem, NCF)
        b = ci % 2
        qt = q % 2
        if q == 0:
            W(T, sw3[b], w3_hist[b][ci // 2])
        if q == 1:
            nxt = ci + 1
            if nxt < NW * E * NCF:
                nw_, nrem = divmod(nxt, E * NCF)
                ne_, nc_ = divmod(nrem, NCF)
                nb = nxt % 2
                if nxt == 1:
                    W(GP, SM["bst"], bst_hist[-1][0])
                    W(GP, SM["bst1"], bst_hist[-1][1])
                    W(GP, SM["tpf"], h2T_done)
                    W(GP, SM["scf"], scf_hist[-1])
                else:
                    W(GP, SM["o3f"], chunk_o3f[nxt - 2])
                wchunk(ne_, nc_, nb)
                w3_hist[nb].append(sw3[nb].v)
        for fsub in range(4):
            it3 = ci * 16 + q * 4 + fsub
            if it3 >= 2:
                W(T, SM["gpf"], gpf_hist[it3 - 2])
            for k in range(8):
                EM(T, lambda t, b=b, k=k, fsub=fsub, w=w, q=q, it3=it3: t.matmul(
                    p3g[it3 % 2][:], wg_sb[b][:, 512 * k + 128 * fsub:512 * k + 128 * (fsub + 1)],
                    h2T[:, NTOK * k + WAVE * w + 512 * q:NTOK * k + WAVE * w + 512 * (q + 1)],
                    start=(k == 0), stop=(k == 7)))
            for k in range(8):
                mmfn = lambda t, b=b, k=k, fsub=fsub, w=w, q=q, it3=it3: t.matmul(
                    p3p[it3 % 2][:], wp_sb[b][:, 512 * k + 128 * fsub:512 * k + 128 * (fsub + 1)],
                    h2T[:, NTOK * k + WAVE * w + 512 * q:NTOK * k + WAVE * w + 512 * (q + 1)],
                    start=(k == 0), stop=(k == 7))
                if k == 7:
                    EM(T, lambda t, f=mmfn: f(t).then_inc(SM["gp"].h, 1))
                else:
                    EM(T, mmfn)
            inc(SM["gp"])
            W(S, SM["gp"])
            if it3 >= 2:
                W(S, SM["gpf"], gpf_hist[it3 - 2])
            EM(S, lambda s, it3=it3: s.activation(rt_sb[it3 % 2][:], p3g[it3 % 2][:],
                                                  AF.Relu).then_inc(SM["rl"].h, 1))
            inc(SM["rl"])
            W(V, SM["rl"])
            if p == 0 and fsub == 0:
                W(V, SM["bst"], bst_hist[-1][0])
                W(V, SM["bst1"], bst_hist[-1][1])
                W(V, SM["skv"], wsel_done)
            if p >= 2:
                W(V, SM["o3f"], pair_o3f[p - 2])
            EM(V, lambda v, qt=qt, fsub=fsub, it3=it3: v.tensor_tensor(
                t_sb[qt][:, 512 * fsub:512 * (fsub + 1)], rt_sb[it3 % 2][:],
                p3p[it3 % 2][:], ALU.mult).then_inc(SM["gpf"].h, 2))
            gpf_hist.append(inc(SM["gpf"], 2))

    def emitB(p):
        ci, q = divmod(p, 4)
        w, rem = divmod(ci, E * NCF)
        e, c = divmod(rem, NCF)
        b = ci % 2
        qt = q % 2
        W(T, SM["gpf"], gpf_hist[ci * 16 + q * 4 + 3])
        for dmh in range(2):
            og = ci * 8 + q * 2 + dmh
            if og >= 1:
                W(T, SM["o3f"], o3f_hist[og - 1])
            for s4 in range(4):
                for fsub in range(4):
                    mmfn = lambda t, qt=qt, s4=s4, fsub=fsub, b=b, dmh=dmh: t.matmul(
                        p3o[s4][:], t_sb[qt][:, 512 * fsub + 128 * s4:512 * fsub + 128 * (s4 + 1)],
                        wo_sb[b][:, DM * fsub + 512 * dmh:DM * fsub + 512 * (dmh + 1)],
                        start=(fsub == 0), stop=(fsub == 3))
                    if fsub == 3:
                        EM(T, lambda t, f=mmfn: f(t).then_inc(SM["o3"].h, 1))
                    else:
                        EM(T, mmfn)
                inc(SM["o3"])
                W(V, SM["o3"])
                if not first_evac_gate_done[w]:
                    W(V, SM["bi"], 256 * (w + 1))
                    first_evac_gate_done[w] = True
                su = q * 4 + s4
                col = (16 * w + su) * 8 + e
                EM(V, lambda v, s4=s4, su=su, col=col, dmh=dmh: v.scalar_tensor_tensor(
                    out_acc[:, DM * su + 512 * dmh:DM * su + 512 * (dmh + 1)],
                    p3o[s4][:], wsel[:, col:col + 1],
                    out_acc[:, DM * su + 512 * dmh:DM * su + 512 * (dmh + 1)],
                    ALU.mult, ALU.add).then_inc(SM["o3f"].h, 1))
                inc(SM["o3f"])
                if e == E - 1 and c == NCF - 1 and dmh == 1:
                    su_done[16 * w + su] = SM["o3f"].v
            o3f_hist.append(SM["o3f"].v)
        pair_o3f.append(SM["o3f"].v)
        if q == 3:
            chunk_o3f.append(SM["o3f"].v)

    NP3 = NW * E * NCF * NQ
    for p in range(NP3):
        emitA(p)
        if p >= 1:
            emitB(p - 1)
    emitB(NP3 - 1)

    for w in range(NW):
        for su in range(16):
            W(SY, SM["o3f"], su_done[16 * w + su])
            dma(SY, out_d.rearrange("(s p) d -> s p d", p=128)[16 * w + su],
                out_acc[:, DM * su:DM * (su + 1)], SM["st"])
            if w == 0:
                W(SY, SM["st"], SM["st"].v)
                dma(SY, out_acc[:, DM * su:DM * (su + 1)],
                    base_d.rearrange("(s p) d -> s p d", p=128)[16 + su], SM["bi"])

    W(T, SM["st"], 256 * NW)

    block = es.enter_context(nc.Block())

    @block.sync
    def _(sync):
        P.run(SY, sync)

    @block.gpsimd
    def _(gp):
        P.run(GP, gp)

    @block.tensor
    def _(t):
        P.run(T, t)

    @block.vector
    def _(v):
        P.run(V, v)

    @block.scalar
    def _(s):
        P.run(S, s)

    es.close()
    return nc


_NC_CACHE = None


def kernel(L0_attn_x, L0_attn_w_qkv, L0_attn_b_qkv, L0_attn_w_proj, L0_attn_b_proj,
           L0_ffn_router_w, L0_ffn_router_b, L0_ffn_temperature,
           L0_ffn_wg, L0_ffn_wp, L0_ffn_wo):
    global _NC_CACHE
    x = np.ascontiguousarray(np.asarray(L0_attn_x, dtype=np.float32))
    assert x.shape == (8, NTOK, DM)
    common = {
        "w_qkv": np.ascontiguousarray(np.asarray(L0_attn_w_qkv, np.float32)),
        "b_qkv": np.ascontiguousarray(np.asarray(L0_attn_b_qkv, np.float32)),
        "w_proj": np.ascontiguousarray(np.asarray(L0_attn_w_proj, np.float32)),
        "b_proj": np.ascontiguousarray(np.asarray(L0_attn_b_proj, np.float32)),
        "router_w": np.ascontiguousarray(np.asarray(L0_ffn_router_w, np.float32)),
        "router_b": np.ascontiguousarray(np.asarray(L0_ffn_router_b, np.float32)),
        "temperature": np.ascontiguousarray(np.asarray(L0_ffn_temperature, np.float32)),
        "wg": np.ascontiguousarray(np.asarray(L0_ffn_wg, np.float32)),
        "wp": np.ascontiguousarray(np.asarray(L0_ffn_wp, np.float32)),
        "wo": np.ascontiguousarray(np.asarray(L0_ffn_wo, np.float32)),
        "identb": np.eye(128, dtype=ml_dtypes.bfloat16),
    }
    if _NC_CACHE is None:
        _NC_CACHE = build_program()
    nc = _NC_CACHE
    in_maps = [dict(common, x=np.ascontiguousarray(x[i])) for i in range(8)]
    res = run_bass_kernel_spmd(nc, in_maps, core_ids=list(range(8)))
    return np.stack([r["out"] for r in res.results], axis=0)


# version 21 (final)
# speedup vs baseline: 1.1692x; 1.0298x over previous
import numpy as np
import ml_dtypes
from contextlib import ExitStack
import concourse.bass as bass
import concourse.mybir as mybir
from concourse.bass_utils import run_bass_kernel_spmd

F32 = mybir.dt.float32
BF16 = mybir.dt.bfloat16
AF = mybir.ActivationFunctionType
ALU = mybir.AluOpType
AX = mybir.AxisListType

NTOK = 4096
DM = 1024
E = 8
FF = 2048
NSUB = NTOK // 128
WAVE = 2048
NW = NTOK // WAVE
NQ = WAVE // 512
NCF = FF // 512
EPS_LN = 1e-5
KVW = 66

TOP = 229344
A0 = TOP - 65536
B0 = A0 - 65536
C0 = B0 - 61440
CW = C0 + 36864

T, V, S, SY, GP = "tensor", "vector", "scalar", "sync", "gpsimd"
ENGS = [T, V, S, SY, GP]


class Sem:
    def __init__(self, nc, es, name):
        self.h = es.enter_context(nc.semaphore(name))
        self.v = 0
        self.name = name


class Prog:
    def __init__(self):
        self.ops = {e: [] for e in ENGS}

    def emit(self, eng, fn):
        self.ops[eng].append(fn)

    def wait(self, eng, sem, val=None):
        v = sem.v if val is None else val
        if v > 0:
            self.ops[eng].append(lambda e, h=sem.h, v=v: e.wait_ge(h, v))

    def run(self, eng, engine_obj):
        for fn in self.ops[eng]:
            fn(engine_obj)


def build_program():
    nc = bass.Bass("TRN2", target_bir_lowering=False, debug=False)
    assert nc.sbuf_top == TOP, nc.sbuf_top

    x_d = nc.dram_tensor("x", [NTOK, DM], F32, kind="ExternalInput").ap()
    wqkv_d = nc.dram_tensor("w_qkv", [DM, 3 * DM], F32, kind="ExternalInput").ap()
    bqkv_d = nc.dram_tensor("b_qkv", [3 * DM], F32, kind="ExternalInput").ap()
    wproj_d = nc.dram_tensor("w_proj", [DM, DM], F32, kind="ExternalInput").ap()
    bproj_d = nc.dram_tensor("b_proj", [DM], F32, kind="ExternalInput").ap()
    rw_d = nc.dram_tensor("router_w", [E, DM], F32, kind="ExternalInput").ap()
    rb_d = nc.dram_tensor("router_b", [E], F32, kind="ExternalInput").ap()
    temp_d = nc.dram_tensor("temperature", [1], F32, kind="ExternalInput").ap()
    wg_d = nc.dram_tensor("wg", [E, DM, FF], F32, kind="ExternalInput").ap()
    wp_d = nc.dram_tensor("wp", [E, DM, FF], F32, kind="ExternalInput").ap()
    wo_d = nc.dram_tensor("wo", [E, FF, DM], F32, kind="ExternalInput").ap()
    identb_d = nc.dram_tensor("identb", [128, 128], BF16, kind="ExternalInput").ap()
    out_d = nc.dram_tensor("out", [NTOK, DM], F32, kind="ExternalOutput").ap()
    base_d = nc.dram_tensor("base_scr", [NTOK, DM], F32, kind="Internal").ap()

    es = ExitStack()
    sb = lambda name, shape, dt=F32: es.enter_context(nc.sbuf_tensor(name, shape, dt))
    man = lambda name, shape, dt, off: nc.alloc_sbuf_tensor_at(name, shape, dt, offset=off)

    identb = sb("identb_s", [128, 128], BF16)
    rwT = sb("rwT", [128, 8 * E], BF16)
    tmp1 = sb("tmp1", [1, 4])
    rtA = sb("rtA", [128, 2])
    rtemp = sb("rtemp", [128, 1])
    bqkv_row = sb("bqkv_row", [1, 3 * DM], BF16)
    bproj_row = sb("bproj_row", [1, DM], BF16)
    rb_row = sb("rb_row", [1, E], BF16)
    onesrow = sb("onesrow", [1, 512], BF16)
    onesf1 = sb("onesf1", [1, 128])
    onesb = sb("onesb", [128, 1], BF16)
    onesf8 = sb("onesf8", [8, 128])
    onescol = sb("onescol", [128, 1])
    eye8 = sb("eye8", [8, 8])
    kvbd = sb("kvbd", [128, 8 * 130], BF16)
    acH = sb("acH", [128, DM], BF16)
    scores = sb("scores", [128, 256])
    wsel = sb("wsel", [128, 256])
    skE = sb("skE", [128, 256])
    sk2 = sb("sk2", [128, 256])
    sk32 = sb("sk32", [128, 32])
    sk32b = sb("sk32b", [128, 32])
    sk32c = sb("sk32c", [128, 32])
    sk32d = sb("sk32d", [128, 32])
    sk8 = sb("sk8", [128, 8])
    skL8 = sb("skL8", [8, 1])
    skL8b = sb("skL8b", [8, 1])
    skdg = sb("skdg", [8, 8])
    z16b = sb("z16b", [128, 16])
    z16c = sb("z16c", [128, 16])
    lnc = sb("lnc", [128, 16])
    mv12 = sb("mv12", [128, 24])
    mv2 = sb("mv2", [128, 4])

    q_sb = man("q_sb", [128, 8 * NTOK], BF16, A0)
    out_acc = man("out_acc", [128, 16 * DM], F32, A0)
    wqkv_sb = man("wqkv_sb", [128, 8 * 3072], BF16, B0)
    xt_sb = [man(f"xt{i}", [128, DM], F32, B0 + 49152 + 4096 * i) for i in range(2)]
    h2T = man("h2T", [128, 8 * NTOK], BF16, B0)
    h1b = [man("h1b0", [128, DM], BF16, C0), man("h1b1", [128, DM], BF16, C0 + 26624)]
    h1T = man("h1T", [128, 8 * 512], BF16, C0 + 2048)
    kb_bf = [man(f"kb{i}", [128, DM], BF16, C0 + 10240 + 2048 * i) for i in range(2)]
    vb_bf = [man(f"vb{i}", [128, DM], BF16, C0 + 14336 + 2048 * i) for i in range(2)]
    qtmp1 = man("qtmp1", [128, 512], F32, C0 + 18432)
    qtmp2 = man("qtmp2", [128, 512], F32, C0 + 20480)
    qtmp3 = man("qtmp3", [128, 512], F32, C0 + 22528)
    rws = man("rws", [E, DM], BF16, C0 + 24576)
    assert C0 + 26624 <= CW
    wproj_sb = man("wproj_sb", [128, 8 * DM], BF16, C0)
    xt2_sb = [man(f"xt2_{i}", [128, DM], F32, C0 + 16384 + 4096 * i) for i in range(2)]
    x1_sb = [man(f"x1_{i}", [128, DM], F32, C0 + 24576 + 4096 * i) for i in range(2)]
    ac_bf = man("ac_bf", [128, DM], BF16, C0 + 32768)
    aT_bf = man("aT_bf", [128, DM], BF16, C0 + 34816)
    wg_sb = [man("wg_p0", [128, 8 * 512], BF16, CW),
             man("wg_p1", [128, 8 * 512], BF16, C0)]
    wp_sb = [man("wp_p0", [128, 8 * 512], BF16, CW + 8192),
             man("wp_p1", [128, 8 * 512], BF16, C0 + 8192)]
    wo_sb = [man("wo_p0", [128, 4 * DM], BF16, CW + 16384),
             man("wo_p1", [128, 4 * DM], BF16, C0 + 16384)]
    t_sb = [man(f"t3_{i}", [128, 4 * 512], BF16, C0 + 24576 + 4096 * i) for i in range(2)]
    rt_sb = [man(f"rt3_{i}", [128, 512], F32, C0 + 32768 + 2048 * i) for i in range(2)]
    assert nc.sbuf_base <= C0, f"bottom region overflow: {nc.sbuf_base} > {C0}"

    esp = ExitStack()
    psf = lambda name, shape, dt=F32: esp.enter_context(nc.psum_tensor(name, shape, dt))
    p1q = [psf(f"p1q{i}", [128, 512]) for i in range(2)]
    p1kv = [psf(f"p1kv{i}", [128, 512]) for i in range(2)]
    p1kvx = [psf(f"p1kvx{i}", [128, 4 * KVW]) for i in range(2)]
    p1tr = [psf(f"p1tr{i}", [128, 128], BF16) for i in range(2)]
    esp.close()
    esp = ExitStack()
    psf = lambda name, shape, dt=F32: esp.enter_context(nc.psum_tensor(name, shape, dt))
    p2at = [psf(f"p2at{i}", [128, 260]) for i in range(2)]
    p2pr = [psf(f"p2pr{i}", [128, 512]) for i in range(2)]
    p2tr = [psf(f"p2tr{i}", [128, 128], BF16) for i in range(2)]
    p2sc = psf("p2sc", [128, 8])
    p2bc = psf("p2bc", [128, 1])
    esp.close()
    esp = ExitStack()
    psf = lambda name, shape, dt=F32: esp.enter_context(nc.psum_tensor(name, shape, dt))
    psk1 = psf("psk1", [8, 1])
    psk2 = psf("psk2", [128, 8])
    esp.close()
    esp = ExitStack()
    psf = lambda name, shape, dt=F32: esp.enter_context(nc.psum_tensor(name, shape, dt))
    p3g = [psf(f"p3g{i}", [128, 512]) for i in range(2)]
    p3p = [psf(f"p3p{i}", [128, 512]) for i in range(2)]
    p3o = [psf(f"p3o{i}", [128, 512]) for i in range(4)]
    esp.close()

    SM = {}
    for nm in ["init", "wqkv", "wproj", "rw", "brow", "x0", "x1p", "x2a", "x2b",
               "xf", "lnA", "lnB", "h1", "tp", "tpf", "q", "qf", "kv", "kvf",
               "kbf", "kvb", "at", "atf", "ac", "pr", "prf", "b2",
               "bst", "bst1", "h2T", "sc", "scf", "skv", "sks", "skt", "sktf",
               "w3a", "w3b", "w3f", "gp", "gpf", "rl", "t3", "t3f", "o3",
               "o3f", "bi", "st", "cst", "vv", "ss", "tpd", "tpd1", "trad", "trhd",
               "wq2", "bi2"]:
        SM[nm] = Sem(nc, es, "s_" + nm)
    sx = [SM["x0"], SM["x1p"]]
    sx2 = [SM["x2a"], SM["x2b"]]
    sbst = [SM["bst"], SM["bst1"]]
    sw3 = [SM["w3a"], SM["w3b"]]

    stpd = None
    P = Prog()
    W = P.wait
    EM = P.emit

    def inc(sem, n=1):
        sem.v += n
        return sem.v

    def dma(eng, out, in_, sem, slow=False, tr=False):
        EM(eng, lambda e, o=out, i=in_, h=sem.h, sl=slow, tr=tr: e.dma_start(
            out=o, in_=i, allow_slow_non_contiguous=sl, transpose=tr).then_inc(h, 16))
        inc(sem, 16)

    def vsem(fn):
        EM(V, lambda v: fn(v).then_inc(SM["vv"].h, 1))
        inc(SM["vv"])
        W(V, SM["vv"])

    def ssem(fn):
        EM(S, lambda s: fn(s).then_inc(SM["ss"].h, 1))
        inc(SM["ss"])
        W(S, SM["ss"])

    dma(SY, identb[:], identb_d, SM["init"])
    dma(SY, tmp1[0:1, 0:1], temp_d.unsqueeze(0), SM["init"])
    dma(GP, bqkv_row[:], bqkv_d.unsqueeze(0), SM["brow"])
    dma(GP, bproj_row[:], bproj_d.unsqueeze(0), SM["brow"])
    dma(GP, rb_row[:], rb_d.unsqueeze(0), SM["brow"])
    dma(GP, rws[:], rw_d, SM["rw"])
    dma(GP, wqkv_sb[:].rearrange("p (k c) -> p k c", c=3072)[:, :, 1024:3072],
        wqkv_d.rearrange("(k p) c -> p k c", p=128)[:, :, 1024:3072], SM["wqkv"])
    dma(GP, wqkv_sb[:].rearrange("p (k c) -> p k c", c=3072)[:, :, 0:1024],
        wqkv_d.rearrange("(k p) c -> p k c", p=128)[:, :, 0:1024], SM["wq2"])

    def wchunk(e, c, b):
        dma(GP, wg_sb[b][:].rearrange("p (k f) -> p k f", f=512),
            wg_d[e].rearrange("(k p) f -> p k f", p=128)[:, :, 512 * c:512 * (c + 1)], sw3[b])
        dma(GP, wp_sb[b][:].rearrange("p (k f) -> p k f", f=512),
            wp_d[e].rearrange("(k p) f -> p k f", p=128)[:, :, 512 * c:512 * (c + 1)], sw3[b])
        dma(GP, wo_sb[b][:].rearrange("p (f d) -> p f d", d=DM),
            wo_d[e].rearrange("(f p) d -> p f d", p=128)[:, 4 * c:4 * (c + 1), :], sw3[b])
    wchunk(0, 0, 0)

    EM(V, lambda v: v.memset(onesrow[:], 1.0))
    EM(V, lambda v: v.memset(onesf1[:], 1.0))
    EM(V, lambda v: v.memset(onesb[:], 1.0))
    EM(V, lambda v: v.memset(onesf8[:], 1.0))
    EM(V, lambda v: v.memset(onescol[:], 1.0))
    EM(V, lambda v: v.memset(p1kvx[0][:], 0.0))
    EM(V, lambda v: v.memset(p1kvx[1][:], 0.0))
    W(V, SM["init"])
    EM(V, lambda v: v.tensor_copy(eye8[:], identb[0:8, 0:8]).then_inc(SM["cst"].h, 1))
    inc(SM["cst"])
    W(T, SM["init"])
    W(T, SM["cst"])
    EM(T, lambda t: t.matmul(p2bc[:], onesf1[:], tmp1[0:1, 0:1], start=True, stop=True).then_inc(SM["skt"].h, 1))
    inc(SM["skt"])
    W(V, SM["skt"])
    vsem(lambda v: v.tensor_copy(rtA[:, 0:1], p2bc[:]))
    vsem(lambda v: v.tensor_scalar(rtA[:, 1:2], rtA[:, 0:1], 0.1, None, ALU.max))
    EM(V, lambda v: v.reciprocal(rtemp[:], rtA[:, 1:2]).then_inc(SM["sktf"].h, 1))
    inc(SM["sktf"])
    W(T, SM["rw"])
    W(T, SM["sktf"])
    for k in range(8):
        if k >= 2:
            W(T, SM["sktf"], SM["sktf"].v - 1)
        EM(T, lambda t, k=k: t.transpose(p1tr[k % 2][:, 0:8], rws[:, 128 * k:128 * (k + 1)],
                                         identb[0:8, 0:8]).then_inc(SM["skt"].h, 1))
        inc(SM["skt"])
        W(V, SM["skt"])
        EM(V, lambda v, k=k: v.tensor_copy(rwT[:, 8 * k:8 * (k + 1)], p1tr[k % 2][:, 0:8]).then_inc(SM["sktf"].h, 1))
        inc(SM["sktf"])

    stpd = [SM["tpd"], SM["tpd1"]]
    tpd_hist = []
    qf_hist = []
    kvf_hist = []
    lnpar = 8

    W(T, SM["wqkv"])
    W(T, SM["brow"], 48)
    W(T, SM["sktf"])

    def p1_ln_v(i):
        pr = i % 2
        if i >= 2:
            W(SY, SM["h1"], i - 1)
        dma(SY, xt_sb[pr][:], x_d.rearrange("(s p) d -> s p d", p=128)[i], sx[pr])
        W(V, sx[pr])
        if i >= 2:
            W(V, SM["h1"], i - 1)
        c0 = lnpar * pr
        m0 = 12 * pr
        EM(V, lambda v, pr=pr, m0=m0: v.bn_stats(mv12[:, m0:m0 + 6], xt_sb[pr][:, 0:512]))
        vsem(lambda v, pr=pr, m0=m0: v.bn_stats(mv12[:, m0 + 6:m0 + 12], xt_sb[pr][:, 512:1024]))
        vsem(lambda v, pr=pr, m0=m0: v.bn_aggr(mv2[:, 2 * pr:2 * pr + 2], mv12[:, m0:m0 + 12]))
        EM(V, lambda v, pr=pr, c0=c0: v.tensor_scalar(lnc[:, c0:c0 + 1], mv2[:, 2 * pr + 1:2 * pr + 2],
                                                      EPS_LN, None, ALU.add).then_inc(SM["lnA"].h, 1))
        inc(SM["lnA"])
        W(S, SM["lnA"])
        ssem(lambda s, c0=c0: s.activation(lnc[:, c0 + 1:c0 + 2], lnc[:, c0:c0 + 1], AF.Ln))
        EM(S, lambda s, c0=c0: s.activation(lnc[:, c0 + 2:c0 + 3], lnc[:, c0 + 1:c0 + 2],
                                            AF.Exp, scale=-0.5).then_inc(SM["lnB"].h, 1))
        inc(SM["lnB"])
        W(S, SM["lnB"])
        W(V, SM["lnB"])
        EM(V, lambda v, c0=c0, pr=pr: v.scalar_tensor_tensor(lnc[:, c0 + 3:c0 + 4], mv2[:, 2 * pr:2 * pr + 1],
                                                             -1.0, lnc[:, c0 + 2:c0 + 3],
                                                             ALU.mult, ALU.mult).then_inc(SM["lnA"].h, 1))
        inc(SM["lnA"])

    def p1_ln_s(i):
        pr = i % 2
        c0 = lnpar * pr
        W(S, SM["lnA"])
        if i >= 2:
            W(S, stpd[tpd_hist[i - 2][0]], tpd_hist[i - 2][1])
        EM(S, lambda s, pr=pr, c0=c0, i=i: s.activation(h1b[i % 2][:], xt_sb[pr][:], AF.Identity,
                                                   bias=lnc[:, c0 + 3:c0 + 4],
                                                   scale=lnc[:, c0 + 2:c0 + 3]).then_inc(SM["h1"].h, 1))
        inc(SM["h1"])

    def p1_tr(i):
        sup, s4 = divmod(i, 4)
        W(SY, SM["h1"], i + 1)
        if sup >= 1:
            W(SY, SM["q"], 8 * sup)
        for j in range(8):
            dma(SY, h1T[:, 512 * j + 128 * s4:512 * j + 128 * (s4 + 1)],
                h1b[i % 2][:, 128 * j:128 * (j + 1)], stpd[i % 2], tr=True)
        tpd_hist.append((i % 2, stpd[i % 2].v))

    def p1_kvkvx(i):
        sup, s4 = divmod(i, 4)
        pr = i % 2
        W(T, stpd[tpd_hist[i][0]], tpd_hist[i][1])
        for cc in range(4):
            kvi = i * 4 + cc
            if kvi >= 2:
                W(T, SM["kvf"], kvf_hist[kvi - 2])
            pb = p1kv[kvi % 2]
            for k in range(8):
                EM(T, lambda t, pb=pb, k=k, s4=s4, cc=cc: t.matmul(
                    pb[:], h1T[:, 512 * k + 128 * s4:512 * k + 128 * (s4 + 1)],
                    wqkv_sb[:, 3072 * k + 1024 + 512 * cc:3072 * k + 1024 + 512 * (cc + 1)],
                    start=(k == 0), stop=False))
            EM(T, lambda t, pb=pb, cc=cc: t.matmul(
                pb[:], onesrow[0:1, 0:128],
                bqkv_row[0:1, 1024 + 512 * cc:1024 + 512 * (cc + 1)],
                start=False, stop=True).then_inc(SM["kv"].h, 1))
            inc(SM["kv"])
            if cc < 2:
                W(V, SM["kv"])
                W(V, SM["sks"])
                EM(V, lambda v, pb=pb, cc=cc: v.tensor_single_scalar(
                    qtmp1[:] if cc == 0 else qtmp2[:], pb[:], 0.0, ALU.min).then_inc(SM["skv"].h, 1))
                inc(SM["skv"])
                W(S, SM["skv"])
                W(S, SM["kvf"])
                W(S, SM["qf"])
                EM(S, lambda s, cc=cc: s.activation(qtmp3[:],
                                                    qtmp1[:] if cc == 0 else qtmp2[:],
                                                    AF.Exp).then_inc(SM["sks"].h, 1))
                inc(SM["sks"])
                W(V, SM["sks"])
                EM(V, lambda v, pb=pb, i=i, cc=cc: v.scalar_tensor_tensor(
                    kb_bf[i % 2][:, 512 * cc:512 * (cc + 1)], pb[:], 0.0, qtmp3[:],
                    ALU.max, ALU.add).then_inc(SM["kvf"].h, 1))
            else:
                W(V, SM["kv"])
                EM(V, lambda v, pb=pb, i=i, cc=cc: v.tensor_copy(
                    vb_bf[i % 2][:, 512 * (cc - 2):512 * (cc - 1)], pb[:]).then_inc(SM["kvf"].h, 1))
            kvf_hist.append(inc(SM["kvf"]))
        W(T, SM["kvf"])
        for pair in range(8):
            g, jj = divmod(pair, 4)
            for half in range(2):
                hd0 = 128 * pair + 64 * half
                last = pair == 7 and half == 1
                EM(T, lambda t, g=g, jj=jj, half=half, hd0=hd0, i=i: t.matmul(
                    p1kvx[g][64 * half:64 * (half + 1), KVW * jj:KVW * jj + 64],
                    kb_bf[i % 2][:, hd0:hd0 + 64], vb_bf[i % 2][:, hd0:hd0 + 64],
                    start=False, stop=False, skip_group_check=True))
                mmfn = lambda t, g=g, jj=jj, half=half, hd0=hd0, i=i: t.matmul(
                    p1kvx[g][64 * half:64 * (half + 1), KVW * jj + 64:KVW * jj + 65],
                    kb_bf[i % 2][:, hd0:hd0 + 64], onesb[:],
                    start=False, stop=False, skip_group_check=True)
                if last:
                    EM(T, lambda t, f=mmfn: f(t).then_inc(SM["kbf"].h, 1))
                else:
                    EM(T, mmfn)
        inc(SM["kbf"])

    def p1_q(sup):
        if sup == 0:
            W(T, SM["wq2"])
        for qc in range(8):
            qi = sup * 8 + qc
            if qi >= 2:
                W(T, SM["qf"], qf_hist[qi - 2])
            for k in range(8):
                EM(T, lambda t, qc=qc, k=k, qi=qi: t.matmul(
                    p1q[qi % 2][:], wqkv_sb[:, 3072 * k + 128 * qc:3072 * k + 128 * (qc + 1)],
                    h1T[:, 512 * k:512 * (k + 1)], start=(k == 0), stop=False))
            EM(T, lambda t, qc=qc, qi=qi: t.matmul(
                p1q[qi % 2][:], bqkv_row[0:1, 128 * qc:128 * (qc + 1)],
                onesrow[0:1, 0:512], start=False, stop=True).then_inc(SM["q"].h, 1))
            inc(SM["q"])
            W(V, SM["q"])
            W(V, SM["sks"])
            EM(V, lambda v, qi=qi, qc=qc: v.tensor_single_scalar(
                qtmp1[:] if qc % 2 == 0 else qtmp2[:], p1q[qi % 2][:], 0.0,
                ALU.min).then_inc(SM["skv"].h, 1))
            inc(SM["skv"])
            W(S, SM["skv"])
            W(S, SM["kvf"])
            W(S, SM["qf"])
            EM(S, lambda s, qc=qc: s.activation(qtmp3[:],
                                                qtmp1[:] if qc % 2 == 0 else qtmp2[:],
                                                AF.Exp).then_inc(SM["sks"].h, 1))
            inc(SM["sks"])
            W(V, SM["sks"])
            EM(V, lambda v, qc=qc, sup=sup, qi=qi: v.scalar_tensor_tensor(
                q_sb[:, NTOK * qc + 512 * sup:NTOK * qc + 512 * (sup + 1)],
                p1q[qi % 2][:], 0.0, qtmp3[:], ALU.max, ALU.add).then_inc(SM["qf"].h, 1))
            qf_hist.append(inc(SM["qf"]))

    p1_ln_v(0)
    p1_ln_s(0)
    p1_ln_v(1)
    p1_ln_s(1)
    for i in range(NSUB):
        if i + 2 < NSUB:
            p1_ln_v(i + 2)
        p1_tr(i)
        if i + 2 < NSUB:
            p1_ln_s(i + 2)
        p1_kvkvx(i)
        if i % 4 == 3:
            p1_q(i // 4)

    W(V, SM["kbf"])
    EM(V, lambda v: v.memset(kvbd[:], 0.0).then_inc(SM["skv"].h, 1))
    inc(SM["skv"])
    W(V, SM["skv"])
    for pair in range(8):
        g, jj = divmod(pair, 4)
        EM(V, lambda v, pair=pair, g=g, jj=jj: v.tensor_copy(
            kvbd[0:64, 130 * pair:130 * pair + 64], p1kvx[g][0:64, KVW * jj:KVW * jj + 64]))
        EM(V, lambda v, pair=pair, g=g, jj=jj: v.tensor_copy(
            kvbd[0:64, 130 * pair + 64:130 * pair + 65], p1kvx[g][0:64, KVW * jj + 64:KVW * jj + 65]))
        EM(V, lambda v, pair=pair, g=g, jj=jj: v.tensor_copy(
            kvbd[64:128, 130 * pair + 65:130 * pair + 129], p1kvx[g][64:128, KVW * jj:KVW * jj + 64]))
        if pair < 7:
            EM(V, lambda v, pair=pair, g=g, jj=jj: v.tensor_copy(
                kvbd[64:128, 130 * pair + 129:130 * pair + 130], p1kvx[g][64:128, KVW * jj + 64:KVW * jj + 65]))
        else:
            EM(V, lambda v, pair=pair, g=g, jj=jj: v.tensor_copy(
                kvbd[64:128, 130 * pair + 129:130 * pair + 130],
                p1kvx[g][64:128, KVW * jj + 64:KVW * jj + 65]).then_inc(SM["kvb"].h, 1))
    inc(SM["kvb"])

    W(GP, SM["qf"], qf_hist[-1])
    W(GP, SM["kbf"])
    dma(GP, wproj_sb[:].rearrange("p (k c) -> p k c", c=DM),
        wproj_d.rearrange("(k p) c -> p k c", p=128), SM["wproj"])

    atf_hist = []
    prf_hist = []
    scf_hist = []
    b2_hist = []
    bst_hist = []
    trad_hist = []
    trhd_hist = []
    p2tpf_hist = []
    p2seq = [0]

    W(T, SM["kvb"])

    def p2_transpose_grp(srcbuf, dst_slices):
        for j in range(8):
            sq = p2seq[0]
            p2seq[0] += 1
            if sq >= 2:
                W(T, SM["tpf"], p2tpf_hist[sq - 2])
            EM(T, lambda t, j=j, sq=sq, srcbuf=srcbuf: t.transpose(
                p2tr[sq % 2][:], srcbuf[:, 128 * j:128 * (j + 1)], identb[:]).then_inc(SM["tp"].h, 1))
            inc(SM["tp"])
            W(V, SM["tp"])
            EM(V, lambda v, j=j, sq=sq, d=dst_slices[j]: v.tensor_copy(
                d, p2tr[sq % 2][:]).then_inc(SM["tpf"].h, 1))
            p2tpf_hist.append(inc(SM["tpf"]))

    def p2_at(i):
        pr = i % 2
        if i < 2:
            W(SY, SM["qf"], qf_hist[-1])
            W(SY, SM["kbf"])
        else:
            W(SY, sbst[pr], bst_hist[i - 2][pr])
        dma(SY, xt2_sb[pr][:], x_d.rearrange("(s p) d -> s p d", p=128)[i], sx2[pr])
        for g4 in range(4):
            agi = i * 4 + g4
            if agi >= 2:
                W(T, SM["atf"], atf_hist[agi - 2])
            for u in range(2):
                pair = g4 * 2 + u
                mmfn = lambda t, g4=g4, u=u, pair=pair, i=i: t.matmul(
                    p2at[g4 % 2][:, 130 * u:130 * (u + 1)],
                    q_sb[:, NTOK * pair + 128 * i:NTOK * pair + 128 * (i + 1)],
                    kvbd[:, 130 * pair:130 * (pair + 1)], start=True, stop=True)
                if u == 1:
                    EM(T, lambda t, f=mmfn: f(t).then_inc(SM["at"].h, 1))
                else:
                    EM(T, mmfn)
            inc(SM["at"])
            W(V, SM["at"])
            if g4 == 0 and i >= 1:
                W(V, SM["tpf"], trad_hist[i - 1])
            pa = p2at[g4 % 2]
            pav = pa[:].rearrange("p (u h c) -> p u h c", h=2, c=65)
            vsem(lambda v, pav=pav, g4=g4: v.tensor_scalar(
                z16b[:, 4 * g4:4 * g4 + 4].rearrange("p (u h) -> p u h", h=2).unsqueeze(3),
                pav[:, :, :, 64:65], 1e-8, None, ALU.add))
            vsem(lambda v, g4=g4: v.reciprocal(z16c[:, 4 * g4:4 * g4 + 4], z16b[:, 4 * g4:4 * g4 + 4]))
            EM(V, lambda v, pav=pav, g4=g4: v.tensor_tensor(
                ac_bf[:, 256 * g4:256 * (g4 + 1)].rearrange("p (u h d) -> p u h d", h=2, d=64),
                pav[:, :, :, 0:64],
                z16c[:, 4 * g4:4 * g4 + 4].rearrange("p (u h) -> p u h", h=2).unsqueeze(3).broadcast_to([128, 2, 2, 64]),
                ALU.mult).then_inc(SM["atf"].h, 1))
            atf_hist.append(inc(SM["atf"]))

    def p2_tra(i):
        W(T, SM["atf"], atf_hist[i * 4 + 3])
        p2_transpose_grp(ac_bf, [aT_bf[:, 128 * j:128 * (j + 1)] for j in range(8)])
        trad_hist.append(SM["tpf"].v)

    def p2_pj(i):
        pr = i % 2
        c0 = lnpar * pr
        m0 = 12 * pr
        if i == 0:
            W(T, SM["wproj"])
        W(T, SM["tpf"], trad_hist[i])
        for dmh in range(2):
            pri = i * 2 + dmh
            if pri >= 2:
                W(T, SM["prf"], prf_hist[pri - 2])
            for k in range(8):
                EM(T, lambda t, k=k, dmh=dmh, pri=pri: t.matmul(
                    p2pr[pri % 2][:], aT_bf[:, 128 * k:128 * (k + 1)],
                    wproj_sb[:, DM * k + 512 * dmh:DM * k + 512 * (dmh + 1)],
                    start=(k == 0), stop=False))
            EM(T, lambda t, dmh=dmh, pri=pri: t.matmul(
                p2pr[pri % 2][:], onesrow[0:1, 0:128],
                bproj_row[0:1, 512 * dmh:512 * (dmh + 1)], start=False, stop=True).then_inc(SM["pr"].h, 1))
            inc(SM["pr"])
            W(V, SM["pr"])
            W(V, sx2[pr])
            if i >= 2:
                W(V, SM["b2"], b2_hist[i - 2])
            EM(V, lambda v, pr=pr, dmh=dmh, pri=pri: v.scalar_tensor_tensor(
                x1_sb[pr][:, 512 * dmh:512 * (dmh + 1)], p2pr[pri % 2][:], 1.0,
                xt2_sb[pr][:, 512 * dmh:512 * (dmh + 1)], ALU.mult, ALU.add).then_inc(SM["prf"].h, 1))
            prf_hist.append(inc(SM["prf"]))
        W(V, SM["prf"])
        EM(V, lambda v, pr=pr, m0=m0: v.bn_stats(mv12[:, m0:m0 + 6], x1_sb[pr][:, 0:512]))
        vsem(lambda v, pr=pr, m0=m0: v.bn_stats(mv12[:, m0 + 6:m0 + 12], x1_sb[pr][:, 512:1024]))
        vsem(lambda v, pr=pr, m0=m0: v.bn_aggr(mv2[:, 2 * pr:2 * pr + 2], mv12[:, m0:m0 + 12]))
        EM(V, lambda v, pr=pr, c0=c0: v.tensor_scalar(lnc[:, c0:c0 + 1], mv2[:, 2 * pr + 1:2 * pr + 2],
                                                      EPS_LN, None, ALU.add).then_inc(SM["lnA"].h, 1))
        inc(SM["lnA"])
        W(S, SM["lnA"])
        ssem(lambda s, c0=c0: s.activation(lnc[:, c0 + 1:c0 + 2], lnc[:, c0:c0 + 1], AF.Ln))
        EM(S, lambda s, c0=c0: s.activation(lnc[:, c0 + 2:c0 + 3], lnc[:, c0 + 1:c0 + 2],
                                            AF.Exp, scale=-0.5).then_inc(SM["lnB"].h, 1))
        inc(SM["lnB"])
        W(S, SM["lnB"])
        W(V, SM["lnB"])
        EM(V, lambda v, c0=c0, pr=pr: v.scalar_tensor_tensor(lnc[:, c0 + 3:c0 + 4], mv2[:, 2 * pr:2 * pr + 1],
                                                             -1.0, lnc[:, c0 + 2:c0 + 3],
                                                             ALU.mult, ALU.mult))
        EM(V, lambda v, c0=c0: v.tensor_scalar(lnc[:, c0 + 4:c0 + 5], lnc[:, c0 + 2:c0 + 3],
                                               1.0, None, ALU.add).then_inc(SM["lnA"].h, 1))
        inc(SM["lnA"])
        W(S, SM["lnA"])
        if i >= 1:
            W(S, SM["tpf"], trhd_hist[i - 1])
        EM(S, lambda s, pr=pr, c0=c0: s.activation(acH[:], x1_sb[pr][:], AF.Identity,
                                                   bias=lnc[:, c0 + 3:c0 + 4],
                                                   scale=lnc[:, c0 + 2:c0 + 3]))
        EM(S, lambda s, pr=pr, c0=c0: s.activation(xt2_sb[pr][:], x1_sb[pr][:], AF.Identity,
                                                   bias=lnc[:, c0 + 3:c0 + 4],
                                                   scale=lnc[:, c0 + 4:c0 + 5]).then_inc(SM["b2"].h, 1))
        b2_hist.append(inc(SM["b2"]))
        W(SY, SM["b2"])
        dma(SY, base_d.rearrange("(s p) d -> s p d", p=128)[i], xt2_sb[pr][:], sbst[pr])
        bst_hist.append((SM["bst"].v, SM["bst1"].v))

    def p2_trh(i):
        W(T, SM["b2"], b2_hist[i])
        p2_transpose_grp(acH, [h2T[:, NTOK * j + 128 * i:NTOK * j + 128 * (i + 1)] for j in range(8)])
        trhd_hist.append(SM["tpf"].v)

    def p2_rt(i):
        W(T, SM["tpf"], trhd_hist[i])
        if i >= 1:
            W(T, SM["scf"], scf_hist[i - 1])
        for k in range(8):
            EM(T, lambda t, k=k, i=i: t.matmul(p2sc[:], h2T[:, NTOK * k + 128 * i:NTOK * k + 128 * (i + 1)],
                                               rwT[:, 8 * k:8 * (k + 1)], start=(k == 0), stop=False))
        EM(T, lambda t: t.matmul(p2sc[:], onesrow[0:1, 0:128], rb_row[:],
                                 start=False, stop=True).then_inc(SM["sc"].h, 1))
        inc(SM["sc"])
        W(V, SM["sc"])
        EM(V, lambda v, i=i: v.tensor_scalar(scores[:, 8 * i:8 * (i + 1)], p2sc[:],
                                             rtemp[:, 0:1], None, ALU.mult).then_inc(SM["scf"].h, 1))
        scf_hist.append(inc(SM["scf"]))

    for i in range(NSUB):
        if i >= 1:
            p2_tra(i - 1)
        if i >= 2:
            p2_trh(i - 2)
            p2_rt(i - 2)
        p2_at(i)
        if i >= 1:
            p2_pj(i - 1)
    p2_tra(NSUB - 1)
    p2_trh(NSUB - 2)
    p2_rt(NSUB - 2)
    p2_pj(NSUB - 1)
    p2_trh(NSUB - 1)
    p2_rt(NSUB - 1)

    r3 = lambda buf: buf[:].rearrange("p (s e) -> p s e", e=8)
    reE = lambda small: small[:].unsqueeze(2).broadcast_to([128, 32, 8])
    reS = lambda small8: small8[:].unsqueeze(1).broadcast_to([128, 32, 8])

    def v2s(fn):
        EM(V, lambda v: fn(v).then_inc(SM["skv"].h, 1))
        inc(SM["skv"])
        W(S, SM["skv"])

    def s2v(fn):
        EM(S, lambda s: fn(s).then_inc(SM["sks"].h, 1))
        inc(SM["sks"])
        W(V, SM["sks"])

    W(V, SM["scf"], scf_hist[-1])
    for it in range(3):
        vsem(lambda v: v.reduce_max(sk32[:], r3(scores), axis=AX.X))
        v2s(lambda v: v.tensor_tensor(r3(skE), r3(scores), reE(sk32), ALU.subtract))
        s2v(lambda s: s.activation(sk2[:], skE[:], AF.Exp))
        v2s(lambda v: v.reduce_sum(sk32b[:], r3(sk2), axis=AX.X))
        s2v(lambda s: s.activation(sk32c[:], sk32b[:], AF.Ln))
        v2s(lambda v: v.tensor_tensor(r3(sk2), r3(skE), reE(sk32c), ALU.subtract))
        s2v(lambda s: s.activation(wsel[:], sk2[:], AF.Exp))
        EM(V, lambda v: v.reduce_sum(sk8[:], wsel[:].rearrange("p (s e) -> p e s", e=8),
                                     axis=AX.X).then_inc(SM["skv"].h, 1))
        inc(SM["skv"])
        W(T, SM["skv"])
        if it >= 1:
            W(T, SM["sktf"])
        EM(T, lambda t: t.matmul(psk1[:], sk8[:], onescol[:], start=True, stop=True).then_inc(SM["skt"].h, 1))
        inc(SM["skt"])
        W(V, SM["skt"])
        v2s(lambda v: v.tensor_copy(skL8[:], psk1[:]))
        s2v(lambda s: s.activation(skL8b[:], skL8[:], AF.Ln))
        EM(V, lambda v: v.tensor_tensor(skdg[:], eye8[:], skL8b[:, 0:1].broadcast_to([8, 8]),
                                        ALU.mult).then_inc(SM["skv"].h, 1))
        inc(SM["skv"])
        W(T, SM["skv"])
        EM(T, lambda t: t.matmul(psk2[:], onesf8[:], skdg[:], start=True, stop=True).then_inc(SM["skt"].h, 1))
        inc(SM["skt"])
        W(V, SM["skt"])
        EM(V, lambda v: v.tensor_copy(sk8[:], psk2[:]).then_inc(SM["sktf"].h, 1))
        inc(SM["sktf"])
        W(V, SM["sktf"])
        EM(V, lambda v: v.tensor_tensor(r3(scores), r3(sk2), reS(sk8), ALU.subtract).then_inc(SM["skv"].h, 1))
        inc(SM["skv"])
        W(S, SM["skv"])
        W(V, SM["skv"])
    s2v(lambda s: s.activation(skE[:], scores[:], AF.Exp))
    vsem(lambda v: v.reduce_sum(sk32[:], r3(skE), axis=AX.X))
    vsem(lambda v: v.tensor_scalar(sk32b[:], sk32[:], 1e-8, None, ALU.add))
    vsem(lambda v: v.reciprocal(sk32c[:], sk32b[:]))
    vsem(lambda v: v.tensor_tensor(r3(sk2), r3(skE), reE(sk32c), ALU.mult))
    vsem(lambda v: v.reduce_max(sk32[:], r3(sk2), axis=AX.X))
    vsem(lambda v: v.tensor_tensor(r3(wsel), r3(sk2), reE(sk32), ALU.is_ge))
    vsem(lambda v: v.scalar_tensor_tensor(r3(skE), r3(wsel), -1e30, r3(sk2),
                                          ALU.mult, ALU.add))
    vsem(lambda v: v.reduce_max(sk32b[:], r3(skE), axis=AX.X))
    vsem(lambda v: v.tensor_tensor(sk32d[:], sk32[:], sk32b[:], ALU.add))
    vsem(lambda v: v.tensor_scalar(sk32[:], sk32d[:], 1e-8, None, ALU.add))
    vsem(lambda v: v.reciprocal(sk32d[:], sk32[:]))
    vsem(lambda v: v.tensor_tensor(r3(wsel), r3(sk2), reE(sk32b), ALU.is_ge))
    vsem(lambda v: v.tensor_tensor(r3(skE), r3(sk2), r3(wsel), ALU.mult))
    EM(V, lambda v: v.tensor_tensor(r3(wsel), r3(skE), reE(sk32d),
                                    ALU.mult).then_inc(SM["skv"].h, 1))
    inc(SM["skv"])
    wsel_done = SM["skv"].v

    h2T_done = trhd_hist[-1]
    su_done = {}
    gpf_hist = []
    pair_o3f = []
    chunk_o3f = []
    o3f_hist = []
    w3_hist = {0: [], 1: []}
    w3_hist[0].append(sw3[0].v)

    W(SY, SM["scf"], scf_hist[-1])
    W(SY, SM["bst"], bst_hist[-1][0])
    W(SY, SM["bst1"], bst_hist[-1][1])
    for su in range(16):
        dma(SY, out_acc[:, DM * su:DM * (su + 1)],
            base_d.rearrange("(s p) d -> s p d", p=128)[su], SM["bi"])
    SY_DEFERRED = True

    W(T, SM["tpf"], h2T_done)
    W(T, SM["sktf"])
    first_evac_gate_done = [False] * NW
    w1_gate_done = [False, False]

    def emitA(p):
        ci, q = divmod(p, 4)
        w, rem = divmod(ci, E * NCF)
        e, c = divmod(rem, NCF)
        b = ci % 2
        qt = q % 2
        if q == 0:
            W(T, sw3[b], w3_hist[b][ci // 2])
        if q == 1:
            nxt = ci + 1
            if nxt < NW * E * NCF:
                nw_, nrem = divmod(nxt, E * NCF)
                ne_, nc_ = divmod(nrem, NCF)
                nb = nxt % 2
                if nxt == 1:
                    W(GP, SM["bst"], bst_hist[-1][0])
                    W(GP, SM["bst1"], bst_hist[-1][1])
                    W(GP, SM["tpf"], h2T_done)
                    W(GP, SM["scf"], scf_hist[-1])
                else:
                    W(GP, SM["o3f"], chunk_o3f[nxt - 2])
                wchunk(ne_, nc_, nb)
                w3_hist[nb].append(sw3[nb].v)
        for fsub in range(4):
            it3 = ci * 16 + q * 4 + fsub
            if it3 >= 2:
                W(T, SM["gpf"], gpf_hist[it3 - 2])
            for k in range(8):
                EM(T, lambda t, b=b, k=k, fsub=fsub, w=w, q=q, it3=it3: t.matmul(
                    p3g[it3 % 2][:], wg_sb[b][:, 512 * k + 128 * fsub:512 * k + 128 * (fsub + 1)],
                    h2T[:, NTOK * k + WAVE * w + 512 * q:NTOK * k + WAVE * w + 512 * (q + 1)],
                    start=(k == 0), stop=(k == 7)))
            for k in range(8):
                mmfn = lambda t, b=b, k=k, fsub=fsub, w=w, q=q, it3=it3: t.matmul(
                    p3p[it3 % 2][:], wp_sb[b][:, 512 * k + 128 * fsub:512 * k + 128 * (fsub + 1)],
                    h2T[:, NTOK * k + WAVE * w + 512 * q:NTOK * k + WAVE * w + 512 * (q + 1)],
                    start=(k == 0), stop=(k == 7))
                if k == 7:
                    EM(T, lambda t, f=mmfn: f(t).then_inc(SM["gp"].h, 1))
                else:
                    EM(T, mmfn)
            inc(SM["gp"])
            W(S, SM["gp"])
            if it3 >= 2:
                W(S, SM["gpf"], gpf_hist[it3 - 2])
            EM(S, lambda s, it3=it3: s.activation(rt_sb[it3 % 2][:], p3g[it3 % 2][:],
                                                  AF.Relu).then_inc(SM["rl"].h, 1))
            inc(SM["rl"])
            W(V, SM["rl"])
            if p == 0 and fsub == 0:
                W(V, SM["bst"], bst_hist[-1][0])
                W(V, SM["bst1"], bst_hist[-1][1])
                W(V, SM["skv"], wsel_done)
            if p >= 2:
                W(V, SM["o3f"], pair_o3f[p - 2])
            EM(V, lambda v, qt=qt, fsub=fsub, it3=it3: v.tensor_tensor(
                t_sb[qt][:, 512 * fsub:512 * (fsub + 1)], rt_sb[it3 % 2][:],
                p3p[it3 % 2][:], ALU.mult).then_inc(SM["gpf"].h, 2))
            gpf_hist.append(inc(SM["gpf"], 2))

    def emitB(p):
        ci, q = divmod(p, 4)
        w, rem = divmod(ci, E * NCF)
        e, c = divmod(rem, NCF)
        b = ci % 2
        qt = q % 2
        W(T, SM["gpf"], gpf_hist[ci * 16 + q * 4 + 3])
        for dmh in range(2):
            og = ci * 8 + q * 2 + dmh
            if og >= 1:
                W(T, SM["o3f"], o3f_hist[og - 1])
            for s4 in range(4):
                for fsub in range(4):
                    mmfn = lambda t, qt=qt, s4=s4, fsub=fsub, b=b, dmh=dmh: t.matmul(
                        p3o[s4][:], t_sb[qt][:, 512 * fsub + 128 * s4:512 * fsub + 128 * (s4 + 1)],
                        wo_sb[b][:, DM * fsub + 512 * dmh:DM * fsub + 512 * (dmh + 1)],
                        start=(fsub == 0), stop=(fsub == 3))
                    if fsub == 3:
                        EM(T, lambda t, f=mmfn: f(t).then_inc(SM["o3"].h, 1))
                    else:
                        EM(T, mmfn)
                inc(SM["o3"])
                W(V, SM["o3"])
                if w == 0:
                    if not first_evac_gate_done[0]:
                        W(V, SM["bi"], 256)
                        first_evac_gate_done[0] = True
                else:
                    half = q // 2
                    if not w1_gate_done[half]:
                        if half == 0:
                            W(V, SM["bi"], 256 + 128)
                        else:
                            W(V, SM["bi2"], 128)
                        w1_gate_done[half] = True
                su = q * 4 + s4
                col = (16 * w + su) * 8 + e
                EM(V, lambda v, s4=s4, su=su, col=col, dmh=dmh: v.scalar_tensor_tensor(
                    out_acc[:, DM * su + 512 * dmh:DM * su + 512 * (dmh + 1)],
                    p3o[s4][:], wsel[:, col:col + 1],
                    out_acc[:, DM * su + 512 * dmh:DM * su + 512 * (dmh + 1)],
                    ALU.mult, ALU.add).then_inc(SM["o3f"].h, 1))
                inc(SM["o3f"])
                if e == E - 1 and c == NCF - 1 and dmh == 1:
                    su_done[16 * w + su] = SM["o3f"].v
            o3f_hist.append(SM["o3f"].v)
        pair_o3f.append(SM["o3f"].v)
        if q == 3:
            chunk_o3f.append(SM["o3f"].v)

    NP3 = NW * E * NCF * NQ
    for p in range(NP3):
        emitA(p)
        if p >= 1:
            emitB(p - 1)
    emitB(NP3 - 1)

    for w in range(NW):
        for su in range(16):
            W(SY, SM["o3f"], su_done[16 * w + su])
            dma(SY, out_d.rearrange("(s p) d -> s p d", p=128)[16 * w + su],
                out_acc[:, DM * su:DM * (su + 1)], SM["st"])
            if w == 0:
                W(SY, SM["st"], SM["st"].v)
                dma(SY, out_acc[:, DM * su:DM * (su + 1)],
                    base_d.rearrange("(s p) d -> s p d", p=128)[16 + su],
                    SM["bi"] if su < 8 else SM["bi2"])

    W(T, SM["st"], 256 * NW)

    block = es.enter_context(nc.Block())

    @block.sync
    def _(sync):
        P.run(SY, sync)

    @block.gpsimd
    def _(gp):
        P.run(GP, gp)

    @block.tensor
    def _(t):
        P.run(T, t)

    @block.vector
    def _(v):
        P.run(V, v)

    @block.scalar
    def _(s):
        P.run(S, s)

    es.close()
    return nc


_NC_CACHE = None


def kernel(L0_attn_x, L0_attn_w_qkv, L0_attn_b_qkv, L0_attn_w_proj, L0_attn_b_proj,
           L0_ffn_router_w, L0_ffn_router_b, L0_ffn_temperature,
           L0_ffn_wg, L0_ffn_wp, L0_ffn_wo):
    global _NC_CACHE
    x = np.ascontiguousarray(np.asarray(L0_attn_x, dtype=np.float32))
    assert x.shape == (8, NTOK, DM)
    common = {
        "w_qkv": np.ascontiguousarray(np.asarray(L0_attn_w_qkv, np.float32)),
        "b_qkv": np.ascontiguousarray(np.asarray(L0_attn_b_qkv, np.float32)),
        "w_proj": np.ascontiguousarray(np.asarray(L0_attn_w_proj, np.float32)),
        "b_proj": np.ascontiguousarray(np.asarray(L0_attn_b_proj, np.float32)),
        "router_w": np.ascontiguousarray(np.asarray(L0_ffn_router_w, np.float32)),
        "router_b": np.ascontiguousarray(np.asarray(L0_ffn_router_b, np.float32)),
        "temperature": np.ascontiguousarray(np.asarray(L0_ffn_temperature, np.float32)),
        "wg": np.ascontiguousarray(np.asarray(L0_ffn_wg, np.float32)),
        "wp": np.ascontiguousarray(np.asarray(L0_ffn_wp, np.float32)),
        "wo": np.ascontiguousarray(np.asarray(L0_ffn_wo, np.float32)),
        "identb": np.eye(128, dtype=ml_dtypes.bfloat16),
    }
    if _NC_CACHE is None:
        _NC_CACHE = build_program()
    nc = _NC_CACHE
    in_maps = [dict(common, x=np.ascontiguousarray(x[i])) for i in range(8)]
    res = run_bass_kernel_spmd(nc, in_maps, core_ids=list(range(8)))
    return np.stack([r["out"] for r in res.results], axis=0)
